# version 19
# baseline (speedup 1.0000x reference)
import numpy as np
import concourse.bass as bass
import concourse.mybir as mybir
from concourse.bass_utils import run_bass_kernel_spmd

F32 = mybir.dt.float32
BF16 = mybir.dt.bfloat16
AF = mybir.ActivationFunctionType
ALU = mybir.AluOpType

NC = 8


class CFG:
    def __init__(self, DCH, HCH, FCH, GCH, T, NT, FGRP, HGRP):
        self.DCH, self.HCH, self.FCH, self.GCH = DCH, HCH, FCH, GCH
        self.D, self.H, self.F, self.G = 128 * DCH, 128 * HCH, 128 * FCH, 128 * GCH
        self.T, self.NT = T, NT
        assert T % NT == 0
        self.TH = T + NT
        self.NWR = T // NT
        self.NWH = self.TH // NT
        self.FGRP, self.HGRP = FGRP, HGRP
        assert FCH % FGRP == 0 and GCH % HGRP == 0
        self.THALF = T // 2 if (T // 2) % NT == 0 else T
        assert self.THALF % NT == 0


CFG_FULL = CFG(DCH=16, HCH=16, FCH=48, GCH=64, T=2048, NT=512, FGRP=4, HGRP=4)


def make_small_cfg():
    return CFG(DCH=2, HCH=2, FCH=3, GCH=4, T=512, NT=512, FGRP=3, HGRP=4)


class Pool:

    def __init__(self, nbytes):
        self.nbytes = nbytes
        self.blocks = {}
        self.free_list = [(0, nbytes)]
        self.peak = 0

    def alloc(self, name, size, top=False):
        size = (size + 511) & ~511
        assert name not in self.blocks, name
        idxs = range(len(self.free_list))
        for i in (reversed(idxs) if top else idxs):
            s, sz = self.free_list[i]
            if sz >= size:
                if top:
                    self.blocks[name] = (s + sz - size, size)
                    self.free_list[i] = (s, sz - size)
                    if sz == size:
                        self.free_list.pop(i)
                    return s + sz - size
                self.blocks[name] = (s, size)
                if sz == size:
                    self.free_list.pop(i)
                else:
                    self.free_list[i] = (s + size, sz - size)
                return s
        raise RuntimeError(
            f"pool OOM for {name} ({size}B): blocks={self.blocks} free={self.free_list}")

    def free(self, name):
        s, size = self.blocks.pop(name)
        self.free_list.append((s, size))
        self.free_list.sort()
        merged = []
        for b in self.free_list:
            if merged and merged[-1][0] + merged[-1][1] == b[0]:
                merged[-1] = (merged[-1][0], merged[-1][1] + b[1])
            else:
                merged.append(b)
        self.free_list = merged

    def off(self, name):
        return self.blocks[name][0]


class Ctx:
    ENGS = ("sync", "gpsimd", "vector", "scalar", "tensor")
    MKEY = {"vector": "dve", "scalar": "act", "tensor": "pe",
            "gpsimd": "pool"}

    def __init__(self, nc, sems):
        self.nc = nc
        self.progs = {e: [] for e in self.ENGS}
        self.sem = sems
        self.n = {k: 0 for k in sems}
        self.w = {}
        self.last = {e: None for e in self.ENGS}
        self.fences = {"sd": [0], "gd": [0], "ad": [0]}
        self.qeng = {"sd": "sync", "gd": "gpsimd", "ad": "scalar"}

    def _append(self, eng, fn, markable):
        self.progs[eng].append(fn)
        if markable:
            self.last[eng] = [len(self.progs[eng]) - 1, fn, False]

    def op(self, eng, fn, key=None):
        if key is None:
            self._append(eng, fn, True)
            return None
        self.n[key] += 1
        sem = self.sem[key]
        self._append(eng, (lambda e, f=fn, s=sem: f(e).then_inc(s, 1)), True)
        self.last[eng][2] = True
        return self.n[key]

    def mark_last(self, eng, key):
        rec = self.last[eng]
        assert rec is not None, f"no instruction to mark on {eng}"
        if rec[2]:
            return self.n[key]
        idx, fn, _ = rec
        self.n[key] += 1
        sem = self.sem[key]
        self.progs[eng][idx] = (lambda e, f=fn, s=sem: f(e).then_inc(s, 1))
        rec[2] = True
        return self.n[key]

    def fence(self, key):
        v = self.n[key]
        if v <= self.fences[key][-1]:
            return v
        eng = self.qeng[key]
        sem = self.sem[key]
        self._append(eng, (lambda e, s=sem, vv=v: e.wait_ge(s, vv)), False)
        self.w[(eng, key)] = max(self.w.get((eng, key), 0), v)
        self.fences[key].append(v)
        return v

    def wait(self, eng, key, val):
        if val is None or val <= 0:
            return
        assert val <= self.n[key], (eng, key, val, self.n[key])
        if key in self.fences:
            cand = [f for f in self.fences[key] if f >= val]
            assert cand, (f"no fence covering {key}>={val} (fences "
                          f"{self.fences[key][-3:]}, n={self.n[key]})")
            val = cand[0]
        if val <= self.w.get((eng, key), 0):
            return
        self.w[(eng, key)] = val
        sem = self.sem[key]
        self._append(eng, (lambda e, s=sem, v=val: e.wait_ge(s, v)), False)

    def dma(self, q, out, in_, key, **kw):
        self.n[key] += 16
        sem = self.sem[key]
        self._append(
            q, (lambda e, o=out, i=in_, s=sem, kw=kw:
                e.dma_start(o, i, **kw).then_inc(s, 16)), False)
        return self.n[key]

    def act(self, out, in_, func, bias=0.0, scale=1.0, key="act"):
        return self.op("scalar",
                       lambda e, o=out, i=in_, f=func, b=bias, s=scale:
                       e.activation(o, i, f, bias=b, scale=s), key)

    def tt(self, out, a, b, alu, key="dve"):
        return self.op("vector",
                       lambda e, o=out, x=a, y=b, al=alu: e.tensor_tensor(o, x, y, al),
                       key)

    def ts(self, out, a, s1, s2, op0, op1=None, key="dve"):
        if op1 is None:
            return self.op("vector",
                           lambda e, o=out, x=a, u=s1, al=op0:
                           e.tensor_single_scalar(o, x, u, al), key)
        return self.op("vector",
                       lambda e, o=out, x=a, u=s1, v=s2, a0=op0, a1=op1:
                       e.tensor_scalar(o, x, u, v, a0, a1), key)

    def stt(self, out, in0, scalar, in1, op0, op1, key="dve"):
        return self.op("vector",
                       lambda e, o=out, x=in0, s=scalar, y=in1, a0=op0, a1=op1:
                       e.scalar_tensor_tensor(o, x, s, y, a0, a1), key)

    def scan(self, out, d0, d1, init, op0, op1, key="dve"):
        return self.op("vector",
                       lambda e, o=out, a=d0, b=d1, i=init, a0=op0, a1=op1:
                       e.tensor_tensor_scan(o, a, b, i, a0, a1), key)

    def copy(self, out, in_, key="dve"):
        return self.op("vector", lambda e, o=out, i=in_: e.tensor_copy(o, i), key)

    def memset(self, ap, v, key="dve"):
        return self.op("vector", lambda e, a=ap, vv=v: e.memset(a, vv), key)

    def recip(self, out, in_, key="dve"):
        return self.op("vector", lambda e, o=out, i=in_: e.reciprocal(o, i), key)

    def ptt(self, out, a, b, alu, key="pool"):
        return self.op("gpsimd",
                       lambda e, o=out, x=a, y=b, al=alu: e.tensor_tensor(o, x, y, al),
                       key)

    def mm(self, out, lhsT, rhs, start, stop, key=None, tile_position=None):
        return self.op("tensor",
                       lambda e, o=out, l=lhsT, r=rhs, s0=start, s1=stop, tp=tile_position:
                       e.matmul(o, l, r, start=s0, stop=s1, tile_position=tp), key)

    def barrier(self):
        self.fence("sd")
        self.fence("gd")
        self.fence("ad")
        for eng in self.ENGS:
            k = self.MKEY.get(eng)
            if k is not None and self.last[eng] is not None and not self.last[eng][2]:
                self.mark_last(eng, k)
        for eng in self.ENGS:
            for k, v in self.n.items():
                if v > 0:
                    self.wait(eng, k, v)


def build_nc(c: CFG, debug=False):
    nc = bass.Bass("TRN2", target_bir_lowering=False)
    D, H, F, G, T, NT, TH = c.D, c.H, c.F, c.G, c.T, c.NT, c.TH
    DCH, HCH, FCH, GCH, NWR, NWH = c.DCH, c.HCH, c.FCH, c.GCH, c.NWR, c.NWH
    assert D == H, "layout assumes d_model == mixer width"

    xfm = nc.dram_tensor("xfm", [D, TH], F32, kind="ExternalInput").ap()
    wT = {}
    for nm, jch, kch in (("w_left", HCH, DCH), ("w_right", HCH, DCH),
                         ("w_ga", HCH, HCH), ("w_gx", HCH, HCH),
                         ("w_out", DCH, HCH), ("w_up1", FCH, DCH),
                         ("w_up2", FCH, DCH), ("w_fup", GCH, DCH)):
        wT[nm] = nc.dram_tensor(nm, [jch, 128, kch * 128], F32,
                                kind="ExternalInput").ap()
    for nm, shp in (("w_down", [F, D]), ("w_fdn", [G, D])):
        wT[nm] = nc.dram_tensor(nm, shp, F32, kind="ExternalInput").ap()
    rows_in = {}
    for nm, n in (("r_n1w", D), ("r_leftb", H), ("r_rightb", H), ("r_convb", H),
                  ("r_gab", H), ("r_gxb", H), ("r_apar", H), ("r_outb", D),
                  ("r_n2w", D), ("r_up1b", F), ("r_up2b", F), ("r_downb", D),
                  ("r_uscale", G), ("r_dscale", D), ("r_oscale", 1)):
        rows_in[nm] = nc.dram_tensor(nm, [1, n], F32, kind="ExternalInput").ap()
    rows_in["r_convw"] = nc.dram_tensor("r_convw", [4, H], F32, kind="ExternalInput").ap()
    halo_mask = nc.dram_tensor("halo_mask", [128, 1], F32, kind="ExternalInput").ap()
    scanm_in = nc.dram_tensor("scanm", [128, NC], F32, kind="ExternalInput").ap()
    scanl_in = nc.dram_tensor("scanl", [128, NC], F32, kind="ExternalInput").ap()

    yfm = nc.dram_tensor("yfm", [D, T], F32, kind="ExternalOutput").ap()

    skind = "ExternalOutput" if debug else "Internal"
    sLEFT = nc.dram_tensor("sLEFT", [H, T], BF16, kind=skind).ap()
    sHL = nc.dram_tensor("sHL", [H, T], BF16, kind=skind).ap()
    sAC = nc.dram_tensor("sAC", [H, T], BF16, kind=skind).ap()
    sX1 = nc.dram_tensor("sX1", [D, T], F32, kind=skind).ap()
    sG = nc.dram_tensor("sG", [F, T], BF16, kind=skind).ap()
    sBO = nc.dram_tensor("sBO", [D, T], F32, kind=skind).ap()
    sOUT1 = nc.dram_tensor("sOUT1", [D, T], F32, kind=skind).ap()
    sHID = nc.dram_tensor("sHID", [G, T], BF16, kind=skind).ap()
    if debug:
        dXN = nc.dram_tensor("dXN", [D, TH], BF16, kind="ExternalOutput").ap()
        dGT = nc.dram_tensor("dGT", [H, T], BF16, kind="ExternalOutput").ap()
        dX1N = nc.dram_tensor("dX1N", [D, T], BF16, kind="ExternalOutput").ap()
        dU2 = nc.dram_tensor("dU2", [D, T], BF16, kind="ExternalOutput").ap()
        dHIN = nc.dram_tensor("dHIN", [128, HCH], F32, kind="ExternalOutput").ap()
    cc_in = nc.dram_tensor("cc_in", [2, 128, HCH], F32)
    cc_out = nc.dram_tensor("cc_out", [NC, 2, 128, HCH], F32)

    from contextlib import ExitStack
    with ExitStack() as es:
        E = es.enter_context
        sems = {k: E(nc.semaphore(k)) for k in ("sd", "gd", "ad", "pe", "dve", "act",
                                                "cc", "pool")}
        ctx = Ctx(nc, sems)

        onesf = E(nc.sbuf_tensor("onesf", [128, 1], F32))
        onesb = E(nc.sbuf_tensor("onesb", [128, 1], BF16))
        ones1f = E(nc.sbuf_tensor("ones1f", [1, 128], F32))
        cEPS5 = E(nc.sbuf_tensor("cEPS5", [128, 1], F32))
        cEPS8 = E(nc.sbuf_tensor("cEPS8", [128, 1], F32))
        cONE = E(nc.sbuf_tensor("cONE", [128, 1], F32))
        hm = E(nc.sbuf_tensor("hm", [128, 1], F32))
        scanm = E(nc.sbuf_tensor("scanm_t", [128, NC], F32))
        scanl = E(nc.sbuf_tensor("scanl_t", [128, NC], F32))
        DVECS = ["r_n1w", "r_leftb", "r_rightb", "r_gab", "r_gxb", "r_apar",
                 "r_outb", "r_n2w", "r_downb", "r_dscale", "r_convb",
                 "cw0", "cw1", "cw2", "cw3"]
        NV = len(DVECS)
        bias_d = E(nc.sbuf_tensor("bias_d", [128, DCH * NV], F32))
        bias_f = E(nc.sbuf_tensor("bias_f", [128, FCH * 2], F32))
        bias_g = E(nc.sbuf_tensor("bias_g", [128, GCH], F32))
        sa_t = E(nc.sbuf_tensor("sa_t", [128, HCH], F32))
        osc = E(nc.sbuf_tensor("osc", [128, 1], F32))
        osrow = E(nc.sbuf_tensor("osrow", [1, 1], F32))
        dsc2 = E(nc.sbuf_tensor("dsc2", [128, DCH], F32))
        acol = E(nc.sbuf_tensor("acol", [128, HCH], F32))
        ecol = E(nc.sbuf_tensor("ecol", [128, HCH], F32))
        agat = E(nc.sbuf_tensor("agat", [128, NC * 2 * HCH], F32))
        hin = E(nc.sbuf_tensor("hin", [128, HCH], F32))
        ctile = E(nc.sbuf_tensor("ctile", [128, HCH], F32))

        PSA = E(nc.psum_tensor("PSA", [128, 2048], F32))
        PSB = E(nc.psum_tensor("PSB", [128, 2048], F32))

        rem = nc.sbuf_bytes_remaining
        pool_bytes = (rem - 2048) & ~1023
        POOL = E(nc.sbuf_tensor("POOL", [128, pool_bytes // 2], BF16))
        pool_f32 = POOL.ap().bitcast(F32)
        pool_bf = POOL.ap()
        pool = Pool(pool_bytes)

        def vf32(name, cols, coloff=0):
            o = pool.off(name) // 4
            return pool_f32[:, o + coloff: o + coloff + cols]

        def vbf(name, cols, coloff=0):
            o = pool.off(name) // 2
            return pool_bf[:, o + coloff: o + coloff + cols]

        def psw(i, rows=None):
            t = PSA if (i * NT) < 2048 else PSB
            col = (i * NT) % 2048
            if rows is None:
                return t[:, col:col + NT]
            return t[rows[0]:rows[1], col:col + NT]

        ctx.memset(onesf[:], 1.0)
        ctx.memset(onesb[:], 1.0)
        ctx.memset(ones1f[:], 1.0)
        ctx.memset(cEPS5[:], 1e-5)
        ctx.memset(cEPS8[:], 1e-8)
        ctx.memset(cONE[:], 1.0)
        ctx.dma("sync", hm[:], halo_mask, "sd")
        ctx.dma("sync", scanm[:], scanm_in, "sd")
        ctx.dma("sync", scanl[:], scanl_in, "sd")
        ctx.dma("sync", osrow[:], rows_in["r_oscale"], "sd")

        ones11 = E(nc.sbuf_tensor("ones11", [1, 1], F32))
        m11 = ctx.memset(ones11[:], 1.0)
        ctx.wait("tensor", "dve", m11)
        stage_sz = max(15 * D, 2 * F, G) * 4
        pool.alloc("stage", stage_sz)

        def stg(p, n):
            return vf32("stage", n, 0)[0:1, :]

        def row_src(nm):
            if nm.startswith("cw"):
                k = int(nm[2])
                return rows_in["r_convw"][k:k + 1, :]
            return rows_in[nm]

        def colize(vecs, dest, ch):
            nvec = len(vecs)
            offs = []
            o = 0
            for nm in vecs:
                offs.append(o)
                o += 128 * ch
            sm = 0
            ctx.wait("scalar", "pe", ctx.n["pe"])
            for vi, nm in enumerate(vecs):
                sm = ctx.dma("scalar", stg(0, o)[0:1, offs[vi]:offs[vi] + 128 * ch],
                             row_src(nm), "ad")
            ctx.fence("ad")
            ctx.wait("tensor", "ad", sm)
            ctx.wait("tensor", "dve", ctx.n["dve"])
            pms = {}
            for vi in range(nvec):
                t = (PSA, PSB)[vi % 2]
                for j in range(ch):
                    ctx.mm(t[:, (vi // 2) * ch + j:(vi // 2) * ch + j + 1],
                           stg(0, o)[0:1, offs[vi] + j * 128:offs[vi] + (j + 1) * 128],
                           ones11[:], start=True, stop=True)
                pms[vi] = ctx.mark_last("tensor", "pe")
            for vi in range(nvec):
                t = (PSA, PSB)[vi % 2]
                ctx.wait("vector", "pe", pms[vi])
                ctx.copy(dest[:, vi * ch:(vi + 1) * ch],
                         t[:, (vi // 2) * ch:(vi // 2) * ch + ch])
            ctx.wait("tensor", "dve", ctx.mark_last("vector", "dve"))
        colize(DVECS, bias_d, DCH)
        colize(["r_up1b", "r_up2b"], bias_f, FCH)
        colize(["r_uscale"], bias_g, GCH)
        pool.free("stage")
        gd_bias = ctx.fence("gd")
        ctx.fence("sd")

        def bd(nm, j):
            v = DVECS.index(nm)
            return bias_d[:, v * DCH + j: v * DCH + j + 1]
        ctx.barrier()

        pool.alloc("xb", 2 * TH * 4)
        pool.alloc("sq", 2 * TH * 2)
        pool.alloc("rows", 4 * TH * 4)
        pool.alloc("B0", T * 4, top=True)
        pool.alloc("B1", T * 4, top=True)
        pool.alloc("B2", TH * 4)
        pool.alloc("B3", TH * 4)

        def xbv(p):
            return vf32("xb", TH, p * TH)

        def sqv(p):
            return vbf("sq", TH, p * TH)

        def rowv(i, cols=None, co=0):
            return vf32("rows", cols if cols is not None else TH, i * TH + co)[0:1, :]

        sq_m, st_m = {}, {}
        for j in range(DCH):
            p = j % 2
            q, qk = (("sync", "sd") if p == 0 else ("gpsimd", "gd"))
            if j >= 2:
                ctx.wait(q, "act", sq_m[j - 2])
                ctx.wait(q, "pe", st_m[j - 2])
                ctx.wait("scalar", "pe", st_m[j - 2])
            xb_m = ctx.dma(q, xbv(p), xfm[j * 128:(j + 1) * 128, :], qk)
            ctx.fence(qk)
            ctx.wait("scalar", qk, xb_m)
            sq_m[j] = ctx.act(sqv(p)[:, 0:T], xbv(p)[:, NT:TH], AF.Square)
            ctx.wait("tensor", "act", sq_m[j])
            ctx.wait("tensor", qk, xb_m)
            for w in range(NWR):
                ctx.mm(PSA[0:1, w * NT:(w + 1) * NT], onesf[:],
                       xbv(p)[:, NT + w * NT:NT + (w + 1) * NT],
                       start=(j == 0), stop=(j == DCH - 1))
                ctx.mm(PSB[0:1, w * NT:(w + 1) * NT], onesb[:],
                       sqv(p)[:, w * NT:(w + 1) * NT],
                       start=(j == 0), stop=(j == DCH - 1))
            st_m[j] = ctx.mark_last("tensor", "pe")
        ctx.wait("vector", "pe", st_m[DCH - 1])
        for w in range(NWR):
            ctx.copy(rowv(0, NT, NT + w * NT), PSA[0:1, w * NT:(w + 1) * NT])
            ctx.copy(rowv(1, NT, NT + w * NT), PSB[0:1, w * NT:(w + 1) * NT])
        dv0 = ctx.mark_last("vector", "dve")
        pe_main = st_m[DCH - 1]
        sqh_m, sth_m = {}, {}
        for j in range(DCH):
            p = j % 2
            q, qk = (("sync", "sd") if p == 0 else ("gpsimd", "gd"))
            if j < 2:
                ctx.wait(q, "pe", pe_main)
            else:
                ctx.wait(q, "act", sqh_m[j - 2])
                ctx.wait(q, "pe", sth_m[j - 2])
                ctx.wait("scalar", "pe", sth_m[j - 2])
            xb_m = ctx.dma(q, xbv(p)[:, 0:NT], xfm[j * 128:(j + 1) * 128, 0:NT], qk)
            ctx.fence(qk)
            ctx.wait("scalar", qk, xb_m)
            sqh_m[j] = ctx.act(sqv(p)[:, 0:NT], xbv(p)[:, 0:NT], AF.Square)
            ctx.wait("tensor", "act", sqh_m[j])
            if j == 0:
                ctx.wait("tensor", "dve", dv0)
            ctx.mm(PSA[0:1, 0:NT], onesf[:], xbv(p)[:, 0:NT],
                   start=(j == 0), stop=(j == DCH - 1))
            ctx.mm(PSB[0:1, 0:NT], onesb[:], sqv(p)[:, 0:NT],
                   start=(j == 0), stop=(j == DCH - 1))
            sth_m[j] = ctx.mark_last("tensor", "pe")
        ctx.wait("vector", "pe", sth_m[DCH - 1])
        ctx.copy(rowv(0, NT, 0), PSA[0:1, 0:NT])
        ctx.copy(rowv(1, NT, 0), PSB[0:1, 0:NT])
        dv = ctx.mark_last("vector", "dve")
        ctx.wait("vector", "dve", dv)

        ctx.ts(rowv(0), rowv(0), 1.0 / D, None, ALU.mult)
        d0 = ctx.ts(rowv(1), rowv(1), 1.0 / D, None, ALU.mult)
        ctx.wait("vector", "dve", d0)
        d1 = ctx.tt(rowv(2), rowv(0), rowv(0), ALU.mult)
        ctx.wait("vector", "dve", d1)
        d2 = ctx.tt(rowv(1), rowv(1), rowv(2), ALU.subtract)
        ctx.wait("scalar", "dve", d2)
        a1 = ctx.act(rowv(2), rowv(1), AF.Sqrt, bias=cEPS5[0:1, 0:1])
        ctx.wait("vector", "act", a1)
        d3 = ctx.recip(rowv(2), rowv(2))
        ctx.wait("vector", "dve", d3)
        d4 = ctx.tt(rowv(3), rowv(1), rowv(2), ALU.mult)
        ctx.wait("vector", "dve", d4)
        d5 = ctx.tt(rowv(3), rowv(3), rowv(2), ALU.mult)
        ctx.wait("scalar", "dve", d5)
        a2 = ctx.act(rowv(3), rowv(3), AF.Sqrt, bias=cEPS8[0:1, 0:1])
        ctx.wait("vector", "act", a2)
        d6 = ctx.recip(rowv(3), rowv(3))
        ctx.wait("vector", "dve", d6)
        d7 = ctx.tt(rowv(3), rowv(3), rowv(2), ALU.mult)
        ctx.wait("vector", "dve", d7)
        ctx.wait("tensor", "dve", d7)

        def bcast(row_ap, nwin, dest_fn):
            cps = {}
            ctx.wait("tensor", "dve", ctx.n["dve"])
            for w in range(nwin):
                t = PSA if (w % 2 == 0) else PSB
                if w >= 2:
                    ctx.wait("tensor", "dve", cps[w - 2])
                pm = ctx.mm(t[:, 1536:1536 + NT], ones1f[:],
                            row_ap[0:1, w * NT:(w + 1) * NT], start=True, stop=True,
                            key="pe")
                ctx.wait("vector", "pe", pm)
                cps[w] = ctx.copy(dest_fn(w), t[:, 1536:1536 + NT])

        bcast(rowv(3), NWH, lambda w: vf32("B2", NT, w * NT))
        d8 = ctx.tt(rowv(3), rowv(3), rowv(0), ALU.mult)
        ctx.wait("vector", "dve", d8)
        ctx.wait("tensor", "dve", d8)
        bcast(rowv(3), NWH, lambda w: vf32("B3", NT, w * NT))
        bcast(rowv(2)[0:1, NT:TH], NWR, lambda w: vf32("B0", NT, w * NT))
        d9 = ctx.tt(rowv(2), rowv(2), rowv(0), ALU.mult)
        ctx.wait("vector", "dve", d9)
        ctx.wait("tensor", "dve", d9)
        bcast(rowv(2)[0:1, NT:TH], NWR, lambda w: vf32("B1", NT, w * NT))

        ctx.wait("vector", "gd", gd_bias)
        vap = DVECS.index("r_apar")
        cp_sa = ctx.copy(sa_t[:, 0:HCH], bias_d[:, vap * DCH:vap * DCH + HCH])
        dvv = ctx.mark_last("vector", "dve")
        ctx.wait("scalar", "dve", dvv)
        aa1 = ctx.act(ctile[:, 0:HCH], sa_t[:, 0:HCH], AF.Abs)
        ctx.wait("scalar", "act", aa1)
        aa2 = ctx.act(ctile[:, 0:HCH], ctile[:, 0:HCH], AF.Exp, scale=-1.0)
        ctx.wait("scalar", "act", aa2)
        aa3 = ctx.act(ctile[:, 0:HCH], ctile[:, 0:HCH], AF.Ln, bias=cONE[:, 0:1])
        ctx.wait("vector", "act", aa3)
        dd1 = ctx.ts(sa_t[:, 0:HCH], sa_t[:, 0:HCH], 0.0, None, ALU.max)
        ctx.wait("vector", "dve", dd1)
        dd2 = ctx.tt(sa_t[:, 0:HCH], sa_t[:, 0:HCH], ctile[:, 0:HCH], ALU.add)
        ctx.wait("vector", "dve", dd2)
        ctx.ts(sa_t[:, 0:HCH], sa_t[:, 0:HCH], -8.0, None, ALU.mult)
        ctx.wait("vector", "sd", ctx.n["sd"])
        pm = ctx.mm(PSB[:, 8:9], ones1f[:], osrow[0:1, 0:1], start=True, stop=True,
                    key="pe")
        ctx.wait("vector", "pe", pm)
        ctx.copy(osc[:], PSB[:, 8:9])
        vds = DVECS.index("r_dscale")
        ctx.copy(dsc2[:, 0:DCH], bias_d[:, vds * DCH:vds * DCH + DCH])
        dvv = ctx.mark_last("vector", "dve")
        ctx.wait("vector", "dve", dvv)
        ctx.ts(dsc2[:, 0:DCH], dsc2[:, 0:DCH], osc[:, 0:1], None, ALU.mult)
        ctx.barrier()
        pool.free("rows")
        pool.free("sq")

        pool.alloc("XN", DCH * TH * 2, top=True)

        def xnv(k, w0=0, cols=None):
            return vbf("XN", cols if cols is not None else TH, k * TH + w0)

        e_m = {}
        for j in range(DCH):
            p = j % 2
            q, qk = (("sync", "sd") if p == 0 else ("gpsimd", "gd"))
            if j >= 2:
                ctx.wait(q, "act", e_m[j - 2])
            xb_m = ctx.dma(q, xbv(p), xfm[j * 128:(j + 1) * 128, :], qk)
            ctx.fence(qk)
            ctx.wait("vector", qk, xb_m)
            t_m = ctx.tt(xbv(p), xbv(p), vf32("B2", TH), ALU.mult)
            ctx.wait("vector", "dve", t_m)
            s_m = ctx.tt(xbv(p), xbv(p), vf32("B3", TH), ALU.subtract)
            ctx.wait("scalar", "dve", s_m)
            e_m[j] = ctx.act(xnv(j), xbv(p), AF.Copy, scale=bd("r_n1w", j))
        ctx.barrier()
        pool.free("B2")
        pool.free("B3")
        pool.free("xb")
        if debug:
            for j in range(DCH):
                ctx.dma("sync", dXN[j * 128:(j + 1) * 128, :], xnv(j), "sd")
            ctx.barrier()

        pool.alloc("rp", TH * 4)
        pool.alloc("cvt", T * 4)
        pool.alloc("wc", 2 * DCH * 128 * 2)
        pool.alloc("CV", HCH * T * 2, top=True)

        def cvv(k, w0=0, cols=None):
            return vbf("CV", cols if cols is not None else T, k * T + w0)

        def wcv(p, k):
            return vbf("wc", 128, p * DCH * 128 + k * 128)

        def load_wcol(wap, j, p, Kch, slot="wc", wf_slot=0, pe_gate=None):
            if pe_gate:
                ctx.wait("gpsimd", "pe", pe_gate)
            ob = pool.off(slot) // 2 + p * Kch * 128
            v = ctx.dma("gpsimd", pool_bf[:, ob:ob + Kch * 128], wap[j], "gd")
            ctx.fence("gd")
            return v

        rp_v = vf32("rp", TH)
        cvt_v = vf32("cvt", T)
        ev, cv_m, pe_m = {}, {}, {}
        for j in range(HCH):
            p = j % 2
            g_m = load_wcol(wT["w_right"], j, p, DCH, "wc", 0,
                            pe_gate=pe_m.get(j - 2))
            ctx.wait("tensor", "gd", g_m)
            if j >= 1:
                ctx.wait("tensor", "act", ev[j - 1])
            for k in range(DCH):
                for w in range(NWH):
                    ctx.mm(psw(w), wcv(p, k), xnv(k, w * NT, NT),
                           start=(k == 0), stop=(k == DCH - 1))
            pe_m[j] = ctx.mark_last("tensor", "pe")
            ctx.wait("scalar", "pe", pe_m[j])
            if j >= 1:
                ctx.wait("scalar", "dve", cv_m[j - 1])
            for w in range(NWH):
                ev[j] = ctx.act(rp_v[:, w * NT:(w + 1) * NT], psw(w), AF.Identity,
                                bias=bd("r_rightb", j))
            ctx.wait("vector", "act", ev[j])
            hm_m = ctx.ts(rp_v[:, NT - 3:NT], rp_v[:, NT - 3:NT], hm[:, 0:1], None,
                          ALU.mult)
            ctx.wait("scalar", "act", ev[j])
            a_m = ctx.act(cvt_v[:], rp_v[:, NT:TH], AF.Identity, scale=bd("cw3", j),
                          bias=bd("r_convb", j))
            ctx.wait("vector", "act", a_m)
            ctx.wait("vector", "dve", hm_m)
            s1 = ctx.stt(cvt_v[:], rp_v[:, NT - 1:TH - 1], bd("cw2", j), cvt_v[:],
                         ALU.mult, ALU.add)
            ctx.wait("vector", "dve", s1)
            s2 = ctx.stt(cvt_v[:], rp_v[:, NT - 2:TH - 2], bd("cw1", j), cvt_v[:],
                         ALU.mult, ALU.add)
            ctx.wait("vector", "dve", s2)
            cv_m[j] = ctx.stt(cvv(j), rp_v[:, NT - 3:TH - 3], bd("cw0", j), cvt_v[:],
                              ALU.mult, ALU.add)
        ctx.barrier()
        pool.free("rp")
        pool.free("cvt")

        pool.alloc("lb", 2 * T * 2)
        lb_st, lev, pel = {}, {}, {}
        for j in range(HCH):
            p = j % 2
            g_m = load_wcol(wT["w_left"], j, p, DCH, "wc", 0, pe_gate=pel.get(j - 2))
            ctx.wait("tensor", "gd", g_m)
            t = (PSA, PSB)[p]
            if j >= 2:
                ctx.wait("tensor", "act", lev[j - 2])
            for k in range(DCH):
                for w in range(NWR):
                    ctx.mm(t[:, w * NT:(w + 1) * NT], wcv(p, k),
                           xnv(k, NT + w * NT, NT), start=(k == 0), stop=(k == DCH - 1))
            pel[j] = ctx.mark_last("tensor", "pe")
            ctx.wait("scalar", "pe", pel[j])
            if j >= 2:
                ctx.wait("scalar", "sd", lb_st[j - 2])
            lev[j] = ctx.act(vbf("lb", T, p * T), t[:, 0:T], AF.Gelu,
                             bias=bd("r_leftb", j))
            ctx.wait("sync", "act", lev[j])
            lb_st[j] = ctx.dma("sync", sLEFT[j * 128:(j + 1) * 128, :],
                               vbf("lb", T, p * T), "sd")
            ctx.fence("sd")
        ctx.barrier()
        pool.free("XN")
        pool.free("lb")
        pool.free("wc")

        pool.alloc("wcga", 2 * HCH * 128 * 2)
        pool.alloc("wcgx", 2 * HCH * 128 * 2)
        for nm in ("e1", "e2", "e3", "e4"):
            pool.alloc(nm, 2 * T * 4)
        for nm in ("e5", "e6", "zeros"):
            pool.alloc(nm, T * 4)
        pool.alloc("e5b", T * 2)
        pool.alloc("e6b", T * 2)
        z_m = ctx.memset(vf32("zeros", T), 0.0)

        def ev_(nm, p):
            return vf32(nm, T, p * T)

        def wcg(slot, p, k):
            return vbf(slot, 128, p * HCH * 128 + k * 128)

        st_hl, sig_i, pega, pegx = {}, {}, {}, {}
        hist = {}
        e5v, e6v = vf32("e5", T), vf32("e6", T)
        for j in range(HCH):
            p = j % 2
            g1 = load_wcol(wT["w_ga"], j, p, HCH, "wcga", 0, pe_gate=pega.get(j - 2))
            g2 = load_wcol(wT["w_gx"], j, p, HCH, "wcgx", 1, pe_gate=pegx.get(j - 2))
            ctx.wait("tensor", "gd", g1)
            if j >= 1:
                ctx.wait("tensor", "act", sig_i[j - 1])
            for k in range(HCH):
                for w in range(NWR):
                    ctx.mm(PSA[:, w * NT:(w + 1) * NT], wcg("wcga", p, k),
                           cvv(k, w * NT, NT), start=(k == 0), stop=(k == HCH - 1))
            pega[j] = ctx.mark_last("tensor", "pe")
            ctx.wait("tensor", "gd", g2)
            for k in range(HCH):
                for w in range(NWR):
                    ctx.mm(PSB[:, w * NT:(w + 1) * NT], wcg("wcgx", p, k),
                           cvv(k, w * NT, NT), start=(k == 0), stop=(k == HCH - 1))
            pegx[j] = ctx.mark_last("tensor", "pe")
            e1, e2, e3, e4 = (ev_("e1", p), ev_("e2", p), ev_("e3", p), ev_("e4", p))
            ctx.wait("scalar", "pe", pega[j])
            if j >= 2:
                ctx.wait("scalar", "dve", hist[("cs", j - 2)])
            r_m = ctx.act(e1, PSA[:, 0:T], AF.Sigmoid, bias=bd("r_gab", j))
            ctx.wait("scalar", "pe", pegx[j])
            if j >= 2:
                ctx.wait("scalar", "dve", hist[("h", j - 2)])
            sig_i[j] = ctx.act(e3, PSB[:, 0:T], AF.Sigmoid, bias=bd("r_gxb", j))
            ctx.wait("scalar", "act", r_m)
            la_m = ctx.act(e1, e1, AF.Copy, scale=sa_t[:, j:j + 1])
            ctx.wait("scalar", "act", la_m)
            a_m = ctx.act(e2, e1, AF.Exp)
            ctx.wait("scalar", "act", a_m)
            if j >= 2:
                ctx.wait("scalar", "dve", hist[("b2", j - 2)])
                ctx.wait("scalar", "act", hist[("mu", j - 2)])
            q_m = ctx.act(e4, e2, AF.Square)
            ctx.wait("vector", "act", q_m)
            q2_m = ctx.ts(e4, e4, 1.0, -1.0, ALU.min, ALU.mult)
            ctx.wait("scalar", "dve", q2_m)
            mu_m = ctx.act(e4, e4, AF.Sqrt, bias=cONE[:, 0:1])
            hist[("mu", j)] = mu_m
            ctx.wait("vector", "act", sig_i[j])
            b1_m = ctx.tt(e3, e3, cvv(j), ALU.mult)
            ctx.wait("vector", "act", mu_m)
            ctx.wait("vector", "dve", b1_m)
            b2_m = ctx.tt(e3, e3, e4, ALU.mult)
            hist[("b2", j)] = b2_m
            ctx.wait("vector", "act", a_m)
            ctx.wait("vector", "dve", b2_m)
            if j >= 1:
                ctx.wait("vector", "dve", hist[("cast", j - 1)])
            h_m = ctx.scan(e5v, e2, e3, 0.0, ALU.mult, ALU.add)
            if j == 0:
                ctx.wait("vector", "dve", z_m)
            ctx.wait("vector", "dve", h_m)
            hist[("h", j)] = h_m
            cs_m = ctx.scan(e6v, e1, vf32("zeros", T), 0.0, ALU.add, ALU.add)
            hist[("cs", j)] = cs_m
            ctx.wait("scalar", "dve", cs_m)
            ctx.wait("scalar", "dve", h_m)
            ac_m = ctx.act(e6v, e6v, AF.Exp)
            ctx.wait("vector", "act", ac_m)
            ctx.copy(acol[:, j:j + 1], e6v[:, T - 1:T])
            ctx.copy(ecol[:, j:j + 1], e5v[:, T - 1:T])
            if j >= 1:
                ctx.wait("vector", "sd", st_hl[j - 1])
            ctx.copy(vbf("e5b", T), e5v)
            cp_m = ctx.copy(vbf("e6b", T), e6v)
            hist[("cast", j)] = cp_m
            ctx.wait("sync", "dve", cp_m)
            ctx.dma("sync", sHL[j * 128:(j + 1) * 128, :], vbf("e5b", T), "sd")
            st_hl[j] = ctx.dma("sync", sAC[j * 128:(j + 1) * 128, :], vbf("e6b", T),
                               "sd")
            ctx.fence("sd")
        ctx.barrier()
        for nm in ("wcga", "wcgx", "e1", "e2", "e3", "e4", "e5", "e6", "zeros",
                   "e5b", "e6b"):
            pool.free(nm)
        pool.free("CV")

        pool.alloc("wc", 2 * HCH * 128 * 2)
        wld6 = {}
        for j6 in (0, 1):
            wld6[j6] = load_wcol(wT["w_out"], j6, j6 % 2, HCH, "wc", 0)
        ctx.dma("sync", cc_in.ap()[0], acol[:, 0:HCH], "sd")
        sd_cc = ctx.dma("sync", cc_in.ap()[1], ecol[:, 0:HCH], "sd")
        ctx.fence("sd")
        pool.alloc("GT", HCH * T * 2, top=True)
        pool.alloc("ldl", 2 * T * 2)
        pool.alloc("ldh", 2 * T * 2)
        pool.alloc("lda", 2 * T * 2)
        pool.alloc("e1", T * 4)
        p5_pre = {}
        for j5 in (0, 1):
            l_m = ctx.dma("sync", vbf("ldl", T, j5 * T),
                          sLEFT[j5 * 128:(j5 + 1) * 128, :], "sd")
            ctx.fence("sd")
            h_m = ctx.dma("scalar", vbf("ldh", T, j5 * T),
                          sHL[j5 * 128:(j5 + 1) * 128, :], "ad")
            ctx.fence("ad")
            a_m = ctx.dma("gpsimd", vbf("lda", T, j5 * T),
                          sAC[j5 * 128:(j5 + 1) * 128, :], "gd")
            ctx.fence("gd")
            p5_pre[j5] = (l_m, h_m, a_m)
        ctx.wait("gpsimd", "sd", sd_cc)
        ctx.op("gpsimd",
               lambda e: e.collective_compute(
                   "AllGather", ALU.bypass, replica_groups=[list(range(NC))],
                   ins=[cc_in.ap().opt()], outs=[cc_out.ap().opt()]), "cc")
        ctx.wait("sync", "cc", 1)
        sd_ag = ctx.dma("sync",
                        agat.ap()[:, 0:NC * 2 * HCH].rearrange(
                            "p (g r c) -> p g r c", g=NC, r=2),
                        cc_out.ap().rearrange("g r p c -> p g r c"), "sd")
        ctx.fence("sd")
        ctx.wait("vector", "sd", sd_ag)
        ctx.memset(ctile[:, 0:HCH], 0.0)
        m1_ = ctx.memset(hin[:, 0:HCH], 0.0)
        ctx.wait("vector", "dve", m1_)
        for g in range(NC):
            Ag = agat[:, (g * 2 + 0) * HCH:(g * 2 + 0) * HCH + HCH]
            Eg = agat[:, (g * 2 + 1) * HCH:(g * 2 + 1) * HCH + HCH]
            t1 = ctx.tt(ctile[:, 0:HCH], ctile[:, 0:HCH], Ag, ALU.mult)
            ctx.wait("vector", "dve", t1)
            t2 = ctx.tt(ctile[:, 0:HCH], ctile[:, 0:HCH], Eg, ALU.add)
            ctx.wait("vector", "dve", t2)
            t3 = ctx.ts(ctile[:, 0:HCH], ctile[:, 0:HCH], scanm[:, g:g + 1], None,
                        ALU.mult)
            ctx.wait("vector", "dve", t3)
            t4 = ctx.stt(hin[:, 0:HCH], ctile[:, 0:HCH], scanl[:, g:g + 1],
                         hin[:, 0:HCH], ALU.mult, ALU.add)
            ctx.wait("vector", "dve", t4)
        if debug:
            ctx.barrier()
            ctx.dma("sync", dHIN, hin[:, 0:HCH], "sd")
        ctx.barrier()

        e1v = vf32("e1", T)

        def gtv(k, w0=0, cols=None):
            return vbf("GT", cols if cols is not None else T, k * T + w0)

        gt_m = {}
        for j in range(HCH):
            p = j % 2
            if j >= 2:
                ctx.wait("sync", "dve", gt_m[j - 2])
                ctx.wait("scalar", "dve", gt_m[j - 2])
                ctx.wait("gpsimd", "dve", gt_m[j - 2])
                l_m = ctx.dma("sync", vbf("ldl", T, p * T),
                              sLEFT[j * 128:(j + 1) * 128, :], "sd")
                ctx.fence("sd")
                h_m = ctx.dma("scalar", vbf("ldh", T, p * T),
                              sHL[j * 128:(j + 1) * 128, :], "ad")
                ctx.fence("ad")
                a_m = ctx.dma("gpsimd", vbf("lda", T, p * T),
                              sAC[j * 128:(j + 1) * 128, :], "gd")
                ctx.fence("gd")
            else:
                l_m, h_m, a_m = p5_pre[j]
            ctx.wait("vector", "ad", h_m)
            ctx.wait("vector", "gd", a_m)
            if j >= 1:
                ctx.wait("vector", "dve", gt_m[j - 1])
            t_m = ctx.stt(e1v, vbf("lda", T, p * T), hin[:, j:j + 1],
                          vbf("ldh", T, p * T), ALU.mult, ALU.add)
            ctx.wait("vector", "sd", l_m)
            ctx.wait("vector", "dve", t_m)
            gt_m[j] = ctx.tt(gtv(j), e1v, vbf("ldl", T, p * T), ALU.mult)
        ctx.barrier()
        for nm in ("ldl", "ldh", "lda", "e1"):
            pool.free(nm)
        if debug:
            for j in range(HCH):
                ctx.dma("sync", dGT[j * 128:(j + 1) * 128, :], gtv(j), "sd")
            ctx.barrier()

        pool.alloc("xb", 2 * TH * 4)
        pool.alloc("e1", T * 4)
        pool.alloc("e2", T * 4)
        pool.alloc("x1b", 2 * T * 4)
        pool.alloc("sqT", 2 * T * 2)
        TH2 = T // 2

        x1_st, ev_m, st6_m, pe6 = {}, {}, {}, {}
        x1_dve, sq6_m = {}, {}

        def p6_sq(qi2):
            if qi2 < 0:
                return
            j2, hf2 = qi2 // 2, qi2 % 2
            h02 = hf2 * TH2
            ctx.wait("scalar", "dve", x1_dve[qi2])
            if qi2 >= 4:
                ctx.wait("scalar", "pe", st6_m[qi2 - 4])
            sq6_m[qi2] = ctx.act(vbf("sqT", TH2, (qi2 % 4) * TH2),
                                 vf32("x1b", TH2, (qi2 % 4) * TH2), AF.Square)
            ctx.wait("tensor", "act", sq6_m[qi2])
            for w in range(2):
                ctx.mm(PSB[0:1, h02 + w * NT:h02 + (w + 1) * NT], onesb[:],
                       vbf("sqT", NT, (qi2 % 4) * TH2 + w * NT), start=(j2 == 0),
                       stop=(j2 == DCH - 1))
            st6_m[qi2] = ctx.mark_last("tensor", "pe")

        for j in range(DCH):
            p = j % 2
            if j >= 2:
                wld6[j] = load_wcol(wT["w_out"], j, p, HCH, "wc", 0,
                                    pe_gate=pe6.get(j - 2))
            xb_m = ctx.dma("sync", xbv(p)[:, 0:T], xfm[j * 128:(j + 1) * 128, NT:TH],
                           "sd")
            ctx.fence("sd")
            ctx.wait("tensor", "gd", wld6[j])
            for hf in range(2):
                qi = j * 2 + hf
                h0 = hf * TH2
                c0 = (qi % 2) * TH2
                if qi >= 2:
                    ctx.wait("tensor", "act", ev_m[qi - 2])
                for k in range(HCH):
                    for w in range(2):
                        ctx.mm(PSA[:, c0 + w * NT:c0 + (w + 1) * NT], wcv(p, k),
                               gtv(k, h0 + w * NT, NT),
                               start=(k == 0), stop=(k == HCH - 1))
                pe6[j] = ctx.mark_last("tensor", "pe")
                p6_sq(qi - 1)
                ctx.wait("scalar", "pe", pe6[j])
                if qi >= 2:
                    ctx.wait("scalar", "dve", x1_dve[qi - 2])
                ev_m[qi] = ctx.act(vf32("e1", TH2, c0), PSA[:, c0:c0 + TH2],
                                   AF.Identity, bias=bd("r_outb", j))
                if hf == 0:
                    ctx.wait("vector", "sd", xb_m)
                if qi >= 2:
                    ctx.wait("vector", "dve", x1_dve[qi - 2])
                if qi >= 4:
                    ctx.wait("vector", "sd", x1_st[qi - 4])
                    ctx.wait("vector", "pe", st6_m[qi - 4])
                    ctx.wait("vector", "act", sq6_m[qi - 4])
                u1 = ctx.tt(vf32("e2", TH2, c0), xbv(p)[:, h0:h0 + TH2],
                            vf32("B0", TH2, h0), ALU.mult)
                ctx.wait("vector", "dve", u1)
                u2 = ctx.tt(vf32("e2", TH2, c0), vf32("e2", TH2, c0),
                            vf32("B1", TH2, h0), ALU.subtract)
                ctx.wait("vector", "act", ev_m[qi])
                ctx.wait("vector", "dve", u2)
                x1_m = ctx.tt(vf32("x1b", TH2, (qi % 4) * TH2), vf32("e1", TH2, c0),
                              vf32("e2", TH2, c0), ALU.add)
                x1_dve[qi] = x1_m
                ctx.wait("sync", "dve", x1_m)
                x1_st[qi] = ctx.dma("sync", sX1[j * 128:(j + 1) * 128, h0:h0 + TH2],
                                    vf32("x1b", TH2, (qi % 4) * TH2), "sd")
                ctx.fence("sd")
        p6_sq(2 * DCH - 1)
        ctx.barrier()
        pool.free("wc")
        pool.free("e1")
        pool.free("e2")
        pool.free("x1b")
        pool.free("sqT")
        pool.free("GT")
        pool.free("B0")
        pool.free("B1")

        pool.alloc("rows", 3 * T * 4)
        pool.alloc("Bi2", T * 4)

        def rowT(i):
            return vf32("rows", T, i * T)[0:1, :]

        cpm = ctx.copy(rowT(0), PSB[0:1, 0:T])
        ctx.wait("vector", "dve", cpm)
        d0 = ctx.ts(rowT(0), rowT(0), 1.0 / H, None, ALU.mult)
        ctx.wait("scalar", "dve", d0)
        a1 = ctx.act(rowT(0), rowT(0), AF.Sqrt, bias=cEPS8[0:1, 0:1])
        ctx.wait("vector", "act", a1)
        d2 = ctx.recip(rowT(0), rowT(0))
        ctx.wait("tensor", "dve", d2)
        ctx.wait("vector", "dve", d2)
        bcast(rowT(0), NWR, lambda w: vf32("Bi2", NT, w * NT))
        ctx.barrier()
        pool.free("rows")

        pool.alloc("X1N", DCH * T * 2, top=True)

        def x1nv(k, w0=0, cols=None):
            return vbf("X1N", cols if cols is not None else T, k * T + w0)

        xe_m = {}
        for j in range(DCH):
            p = j % 2
            q, qk = (("sync", "sd") if p == 0 else ("gpsimd", "gd"))
            if j >= 2:
                ctx.wait(q, "act", xe_m[j - 2])
            sd_m = ctx.dma(q, xbv(p)[:, 0:T], sX1[j * 128:(j + 1) * 128, :], qk)
            ctx.fence(qk)
            ctx.wait("vector", qk, sd_m)
            t_m = ctx.tt(xbv(p)[:, 0:T], xbv(p)[:, 0:T], vf32("Bi2", T), ALU.mult)
            ctx.wait("scalar", "dve", t_m)
            xe_m[j] = ctx.act(x1nv(j), xbv(p)[:, 0:T], AF.Copy, scale=bd("r_n2w", j))
        ctx.barrier()
        pool.free("Bi2")
        pool.free("xb")
        if debug:
            for j in range(DCH):
                ctx.dma("sync", dX1N[j * 128:(j + 1) * 128, :], x1nv(j), "sd")
            ctx.barrier()

        pool.alloc("wcu1", 2 * DCH * 128 * 2)
        pool.alloc("wcu2", 2 * DCH * 128 * 2)
        pool.alloc("e1", T * 4)
        pool.alloc("e2", T * 4)
        pool.alloc("gb", 2 * T * 2)

        def wcn(nm, p, k):
            return vbf(nm, 128, p * DCH * 128 + k * 128)

        g_st, gel_m, lin1_m, gg_m7, pe7u2 = {}, {}, {}, {}, {}
        for f in range(FCH):
            p = f % 2
            g1_m = load_wcol(wT["w_up1"], f, p, DCH, "wcu1", 0,
                             pe_gate=pe7u2.get(f - 2))
            g2_m = load_wcol(wT["w_up2"], f, p, DCH, "wcu2", 1,
                             pe_gate=pe7u2.get(f - 2))
            ctx.wait("tensor", "gd", g1_m)
            ctx.wait("tensor", "gd", g2_m)
            for hf in range(2):
                qi = f * 2 + hf
                bank = (PSA, PSB)[qi % 2]
                h0 = hf * TH2
                if qi >= 2:
                    ctx.wait("tensor", "act", gel_m[qi - 2])
                    ctx.wait("tensor", "dve", lin1_m[qi - 2])
                for k in range(DCH):
                    for w in range(2):
                        ctx.mm(bank[:, w * NT:(w + 1) * NT], wcn("wcu1", p, k),
                               x1nv(k, h0 + w * NT, NT), start=(k == 0),
                               stop=(k == DCH - 1))
                pm1 = ctx.mark_last("tensor", "pe")
                for k in range(DCH):
                    for w in range(2):
                        ctx.mm(bank[:, TH2 + w * NT:TH2 + (w + 1) * NT],
                               wcn("wcu2", p, k), x1nv(k, h0 + w * NT, NT),
                               start=(k == 0), stop=(k == DCH - 1))
                pm2 = ctx.mark_last("tensor", "pe")
                pe7u2[f] = pm2
                e1h = vf32("e1", TH2, (qi % 2) * TH2)
                e2h = vf32("e2", TH2, (qi % 2) * TH2)
                ctx.wait("scalar", "pe", pm1)
                if qi >= 2:
                    ctx.wait("scalar", "dve", gg_m7[qi - 2])
                gel_m[qi] = ctx.act(e1h, bank[:, 0:TH2], AF.Gelu,
                                    bias=bias_f[:, f:f + 1])
                ctx.wait("vector", "pe", pm2)
                lin1_m[qi] = ctx.ts(e2h, bank[:, TH2:T],
                                    bias_f[:, FCH + f:FCH + f + 1], None, ALU.add)
                if qi >= 2:
                    ctx.wait("vector", "sd", g_st[qi - 2])
                ctx.wait("vector", "act", gel_m[qi])
                gg_m7[qi] = ctx.tt(vbf("gb", TH2, (qi % 2) * T + h0), e1h, e2h,
                                   ALU.mult)
                ctx.wait("sync", "dve", gg_m7[qi])
                g_st[qi] = ctx.dma("sync", sG[f * 128:(f + 1) * 128, h0:h0 + TH2],
                                   vbf("gb", TH2, (qi % 2) * T + h0), "sd")
                ctx.fence("sd")
        ctx.barrier()
        for nm in ("wcu1", "wcu2", "e1", "e2", "gb"):
            pool.free(nm)
        pool.free("X1N")

        pool.alloc("OUT", DCH * T * 4, top=True)
        FG = c.FGRP
        NGRP = FCH // FG
        pool.alloc("wstr", 2 * FG * DCH * 128 * 2)
        pool.alloc("gstr", 2 * FG * c.THALF * 2)

        def outv(k, w0=0, cols=None):
            return vf32("OUT", cols if cols is not None else T, k * T + w0)

        wld8, grp_pe8 = {}, {}

        def p8_load(g):
            if g >= NGRP:
                return
            s = g % 2
            ctx.wait("gpsimd", "pe", grp_pe8.get(g - 2, 0))
            for i in range(FG):
                f = g * FG + i
                o = pool.off("wstr") // 2 + (s * FG + i) * DCH * 128
                ctx.dma("gpsimd", pool_bf[:, o:o + DCH * 128],
                        wT["w_down"][f * 128:(f + 1) * 128, :], "gd")
            ctx.fence("gd")
            wld8[g] = ctx.n["gd"]

        p8_load(0)
        p8_load(1)
        pe8_hist = {}
        slot_add8 = {}
        it8 = 0
        for grp in range(NGRP):
            s = grp % 2
            ctx.wait("tensor", "gd", wld8[grp])
            for half in range(T // c.THALF):
                h0 = half * c.THALF
                gp = it8 % 2
                if it8 >= 2:
                    ctx.wait("sync", "pe", pe8_hist[it8 - 2])
                for i in range(FG):
                    f = grp * FG + i
                    o = pool.off("gstr") // 2 + (gp * FG + i) * c.THALF
                    gg_m = ctx.dma("sync", pool_bf[:, o:o + c.THALF],
                                   sG[f * 128:(f + 1) * 128, h0:h0 + c.THALF], "sd")
                ctx.fence("sd")
                ctx.wait("tensor", "sd", gg_m)
                nw = c.THALF // NT
                for d in range(DCH):
                    t = (PSA, PSB)[d % 2]
                    ctx.wait("tensor", "dve", slot_add8.get(d % 2, 0))
                    for i in range(FG):
                        for w in range(nw):
                            ctx.mm(t[:, w * NT:(w + 1) * NT],
                                   vbf("wstr", 128, (s * FG + i) * DCH * 128 + d * 128),
                                   vbf("gstr", NT, (gp * FG + i) * c.THALF + w * NT),
                                   start=(i == 0), stop=(i == FG - 1))
                    pm = ctx.mark_last("tensor", "pe")
                    ctx.wait("vector", "pe", pm)
                    if grp == 0:
                        prev_add = ctx.copy(outv(d, h0, c.THALF), t[:, 0:c.THALF])
                    else:
                        prev_add = ctx.tt(outv(d, h0, c.THALF), outv(d, h0, c.THALF),
                                          t[:, 0:c.THALF], ALU.add)
                    slot_add8[d % 2] = prev_add
                pe8_hist[it8] = ctx.n["pe"]
                it8 += 1
            grp_pe8[grp] = ctx.n["pe"]
            p8_load(grp + 2)
        ctx.barrier()
        pool.free("wstr")
        pool.free("gstr")
        pool.alloc("xb", 2 * TH * 4)
        pool.alloc("sq2", 2 * T * 2)

        def sqT2(p):
            return vbf("sq2", T, p * T)

        bo_dve, st8, sq8 = {}, {}, {}

        def bo_sq(d2):
            if d2 < 0:
                return
            p2 = d2 % 2
            ctx.wait("scalar", "dve", bo_dve[d2])
            if d2 >= 2:
                ctx.wait("scalar", "pe", st8[d2 - 2])
            sq8[d2] = ctx.act(sqT2(p2), outv(d2), AF.Square)
            ctx.wait("tensor", "act", sq8[d2])
            ctx.wait("tensor", "dve", bo_dve[d2])
            for w in range(NWR):
                ctx.mm(PSA[0:1, w * NT:(w + 1) * NT], onesf[:],
                       outv(d2, w * NT, NT), start=(d2 == 0), stop=(d2 == DCH - 1))
                ctx.mm(PSB[0:1, w * NT:(w + 1) * NT], onesb[:],
                       sqT2(p2)[:, w * NT:(w + 1) * NT], start=(d2 == 0),
                       stop=(d2 == DCH - 1))
            st8[d2] = ctx.mark_last("tensor", "pe")

        for d in range(DCH):
            p = d % 2
            q, qk = (("sync", "sd") if p == 0 else ("gpsimd", "gd"))
            if d >= 2:
                ctx.wait(q, "dve", bo_dve[d - 2])
            xb_m = ctx.dma(q, xbv(p)[:, 0:T], sX1[d * 128:(d + 1) * 128, :], qk)
            ctx.fence(qk)
            a_m = ctx.act(outv(d), outv(d), AF.Identity, bias=bd("r_downb", d))
            ctx.wait("vector", "act", a_m)
            ctx.wait("vector", qk, xb_m)
            b_m = ctx.tt(outv(d), outv(d), xbv(p)[:, 0:T], ALU.add)
            bo_dve[d] = b_m
            bo_sq(d - 1)
        bo_sq(DCH - 1)
        ctx.barrier()

        pool.alloc("rows", 3 * T * 4)
        pool.alloc("Br", T * 4)
        pool.alloc("Bm", T * 4)

        def ln_rows_and_bcast(denom):
            ctx.ts(rowT(0), rowT(0), 1.0 / denom, None, ALU.mult)
            d0 = ctx.ts(rowT(1), rowT(1), 1.0 / denom, None, ALU.mult)
            ctx.wait("vector", "dve", d0)
            d1 = ctx.tt(rowT(2), rowT(0), rowT(0), ALU.mult)
            ctx.wait("vector", "dve", d1)
            d2 = ctx.tt(rowT(1), rowT(1), rowT(2), ALU.subtract)
            ctx.wait("scalar", "dve", d2)
            a1 = ctx.act(rowT(1), rowT(1), AF.Sqrt, bias=cEPS5[0:1, 0:1])
            ctx.wait("vector", "act", a1)
            d3 = ctx.recip(rowT(1), rowT(1))
            ctx.wait("vector", "dve", d3)
            ctx.wait("tensor", "dve", d3)
            bcast(rowT(1), NWR, lambda w: vf32("Br", NT, w * NT))
            d4 = ctx.tt(rowT(2), rowT(0), rowT(1), ALU.mult)
            ctx.wait("vector", "dve", d4)
            ctx.wait("tensor", "dve", d4)
            bcast(rowT(2), NWR, lambda w: vf32("Bm", NT, w * NT))

        ctx.copy(rowT(0), PSA[0:1, 0:T])
        dv = ctx.copy(rowT(1), PSB[0:1, 0:T])
        ctx.wait("vector", "dve", dv)
        ln_rows_and_bcast(D)
        ctx.barrier()

        o1_st, st9, o1_dve, sq9 = {}, {}, {}, {}

        def p9_sq(j2):
            if j2 < 0:
                return
            p2 = j2 % 2
            ctx.wait("scalar", "dve", o1_dve[j2])
            if j2 >= 2:
                ctx.wait("scalar", "pe", st9[j2 - 2])
            sq9[j2] = ctx.act(sqT2(p2), outv(j2), AF.Square)
            ctx.wait("tensor", "act", sq9[j2])
            ctx.wait("tensor", "dve", o1_dve[j2])
            for w in range(NWR):
                ctx.mm(PSA[0:1, w * NT:(w + 1) * NT], onesf[:],
                       outv(j2, w * NT, NT), start=(j2 == 0), stop=(j2 == DCH - 1))
                ctx.mm(PSB[0:1, w * NT:(w + 1) * NT], onesb[:],
                       sqT2(p2)[:, w * NT:(w + 1) * NT], start=(j2 == 0),
                       stop=(j2 == DCH - 1))
            st9[j2] = ctx.mark_last("tensor", "pe")

        for j in range(DCH):
            p = j % 2
            q, qk = (("scalar", "ad") if p == 0 else ("gpsimd", "gd"))
            if j >= 2:
                ctx.wait(q, "dve", o1_dve[j - 2])
            x_m = ctx.dma(q, xbv(p)[:, 0:T], xfm[j * 128:(j + 1) * 128, NT:TH], qk)
            ctx.fence(qk)
            t1 = ctx.tt(outv(j), outv(j), vf32("Br", T), ALU.mult)
            ctx.wait("vector", "dve", t1)
            t2 = ctx.tt(outv(j), outv(j), vf32("Bm", T), ALU.subtract)
            ctx.wait("vector", qk, x_m)
            ctx.wait("vector", "dve", t2)
            t3 = ctx.tt(outv(j), outv(j), xbv(p)[:, 0:T], ALU.add)
            o1_dve[j] = t3
            ctx.wait("sync", "dve", t3)
            o1_st[j] = ctx.dma("sync", sOUT1[j * 128:(j + 1) * 128, :], outv(j), "sd")
            ctx.fence("sd")
            p9_sq(j - 1)
        p9_sq(DCH - 1)
        ctx.wait("vector", "pe", st9[DCH - 1])
        ctx.copy(rowT(0), PSA[0:1, 0:T])
        dv = ctx.copy(rowT(1), PSB[0:1, 0:T])
        ctx.wait("vector", "dve", dv)
        ln_rows_and_bcast(D)
        ctx.barrier()
        pool.free("sq2")
        pool.free("xb")

        pool.alloc("scr", T * 4)

        def u2v(k, w0=0, cols=None):
            o = pool.off("OUT") // 2 + k * T * 2
            n = cols if cols is not None else T
            return pool_bf[:, o + w0: o + w0 + n]

        u2_m = {}
        for j in range(DCH):
            if j >= 1:
                ctx.wait("vector", "dve", u2_m[j - 1])
            t1 = ctx.tt(vf32("scr", T), outv(j), vf32("Br", T), ALU.mult)
            ctx.wait("vector", "dve", t1)
            u2_m[j] = ctx.tt(u2v(j), vf32("scr", T), vf32("Bm", T), ALU.subtract)
        ctx.barrier()
        pool.free("Br")
        pool.free("Bm")
        pool.free("rows")
        pool.free("scr")

        pool.alloc("wcf", 2 * DCH * 128 * 4)
        pool.alloc("wcs", 2 * DCH * 128 * 2)
        pool.alloc("hb2", 2 * T * 2)
        hid_st, sgn_m, pm_m, hev_m, wld = {}, {}, {}, {}, {}

        def p11_load(h):
            if h >= GCH:
                return
            p = h % 2
            o = pool.off("wcf") // 4 + p * DCH * 128
            if h >= 2:
                ctx.wait("sync", "act", sgn_m[h - 2])
            wld[h] = ctx.dma("sync", pool_f32[:, o:o + DCH * 128], wT["w_fup"][h], "sd")
            ctx.fence("sd")

        def p11_sign(h):
            if h >= GCH:
                return
            p = h % 2
            ctx.wait("scalar", "sd", wld[h])
            if h >= 2:
                ctx.wait("scalar", "pe", pm_m[h - 2])
            sgn_m[h] = ctx.act(vbf("wcs", DCH * 128, p * DCH * 128),
                               vf32("wcf", DCH * 128, p * DCH * 128), AF.Sign)

        p11_load(0)
        p11_sign(0)
        p11_load(1)
        for h in range(GCH):
            p = h % 2
            ctx.wait("tensor", "act", sgn_m[h])
            t = (PSA, PSB)[p]
            if h >= 2:
                ctx.wait("tensor", "act", hev_m[h - 2])
            for k in range(DCH):
                for w in range(NWR):
                    ctx.mm(t[:, w * NT:(w + 1) * NT],
                           vbf("wcs", 128, p * DCH * 128 + k * 128),
                           u2v(k, w * NT, NT), start=(k == 0), stop=(k == DCH - 1))
            pm_m[h] = ctx.mark_last("tensor", "pe")
            p11_sign(h + 1)
            p11_load(h + 2)
            ctx.wait("scalar", "pe", pm_m[h])
            if h >= 2:
                ctx.wait("scalar", "sd", hid_st[h - 2])
            hev_m[h] = ctx.act(vbf("hb2", T, p * T), t[:, 0:T], AF.Relu,
                               scale=bias_g[:, h:h + 1])
            ctx.wait("sync", "act", hev_m[h])
            hid_st[h] = ctx.dma("sync", sHID[h * 128:(h + 1) * 128, :],
                                vbf("hb2", T, p * T), "sd")
            ctx.fence("sd")
        ctx.barrier()
        for nm in ("wcf", "wcs", "hb2", "OUT"):
            pool.free(nm)

        pool.alloc("OUT", DCH * T * 4, top=True)
        HG = c.HGRP
        NG2 = GCH // HG
        TQ12 = 2 * NT
        pool.alloc("wstr", 2 * HG * DCH * 128 * 2)
        pool.alloc("wstg", 2 * DCH * 128 * 4)
        pool.alloc("hstr", 2 * HG * TQ12 * 2)

        wld12, sgn12, grp_pe12 = {}, {}, {}
        nls = [0, 0]

        def p12_load():
            hh = nls[0]
            if hh >= GCH:
                return
            s = hh % 2
            ctx.wait("sync", "act", sgn12.get(hh - 2, 0))
            wld12[hh] = ctx.dma("sync", vf32("wstg", DCH * 128, s * DCH * 128),
                                wT["w_fdn"][hh * 128:(hh + 1) * 128, :], "sd")
            ctx.fence("sd")
            nls[0] += 1

        def p12_sign():
            hh = nls[1]
            if hh >= GCH:
                return
            while nls[0] < min(hh + 2, GCH):
                p12_load()
            g, i = hh // HG, hh % HG
            s2 = g % 2
            ctx.wait("scalar", "sd", wld12[hh])
            ctx.wait("scalar", "pe", grp_pe12.get(g - 2, 0))
            sgn12[hh] = ctx.act(vbf("wstr", DCH * 128, (s2 * HG + i) * DCH * 128),
                                vf32("wstg", DCH * 128, (hh % 2) * DCH * 128), AF.Sign)
            nls[1] += 1

        p12_load()
        p12_load()
        for _ in range(HG):
            p12_sign()
        pe12_hist = {}
        slot_add12 = {}
        it12 = 0
        for grp in range(NG2):
            s2 = grp % 2
            ctx.wait("tensor", "act", sgn12[grp * HG + HG - 1])
            for q in range(T // TQ12):
                h0 = q * TQ12
                gp = it12 % 2
                if it12 >= 2:
                    ctx.wait("sync", "pe", pe12_hist[it12 - 2])
                for i in range(HG):
                    hh = grp * HG + i
                    o = pool.off("hstr") // 2 + (gp * HG + i) * TQ12
                    hh_m = ctx.dma("sync", pool_bf[:, o:o + TQ12],
                                   sHID[hh * 128:(hh + 1) * 128, h0:h0 + TQ12], "sd")
                ctx.fence("sd")
                ctx.wait("tensor", "sd", hh_m)
                for d in range(DCH):
                    t = (PSA, PSB)[d % 2]
                    ctx.wait("tensor", "dve", slot_add12.get(d % 2, 0))
                    for i in range(HG):
                        for w in range(TQ12 // NT):
                            ctx.mm(t[:, w * NT:(w + 1) * NT],
                                   vbf("wstr", 128, (s2 * HG + i) * DCH * 128 + d * 128),
                                   vbf("hstr", NT, (gp * HG + i) * TQ12 + w * NT),
                                   start=(i == 0), stop=(i == HG - 1))
                    pm = ctx.mark_last("tensor", "pe")
                    ctx.wait("vector", "pe", pm)
                    if grp == 0:
                        prev_add = ctx.copy(outv(d, h0, TQ12), t[:, 0:TQ12])
                    else:
                        prev_add = ctx.tt(outv(d, h0, TQ12), outv(d, h0, TQ12),
                                          t[:, 0:TQ12], ALU.add)
                    slot_add12[d % 2] = prev_add
                pe12_hist[it12] = ctx.n["pe"]
                it12 += 1
                p12_sign()
                p12_sign()
            grp_pe12[grp] = ctx.n["pe"]

        ctx.barrier()
        pool.free("wstr")
        pool.free("wstg")
        pool.free("hstr")
        pool.alloc("rows", 3 * T * 4)
        pool.alloc("sq2", 2 * T * 2)
        pool.alloc("Br", T * 4)
        pool.alloc("Bm", T * 4)
        stf, sqf, fo_m = {}, {}, {}
        for d in range(DCH):
            p = d % 2
            if d >= 2:
                ctx.wait("scalar", "pe", stf[d - 2])
                ctx.wait("vector", "pe", stf[d - 2])
                ctx.wait("vector", "act", sqf[d - 2])
            f_m = ctx.ts(outv(d), outv(d), dsc2[:, d:d + 1], None, ALU.mult)
            fo_m[d] = f_m
            ctx.wait("scalar", "dve", f_m)
            sqf[d] = ctx.act(sqT2(p), outv(d), AF.Square)
            ctx.wait("tensor", "act", sqf[d])
            ctx.wait("tensor", "dve", f_m)
            for w in range(NWR):
                ctx.mm(PSA[0:1, w * NT:(w + 1) * NT], onesf[:], outv(d, w * NT, NT),
                       start=(d == 0), stop=(d == DCH - 1))
                ctx.mm(PSB[0:1, w * NT:(w + 1) * NT], onesb[:],
                       sqT2(p)[:, w * NT:(w + 1) * NT], start=(d == 0),
                       stop=(d == DCH - 1))
            stf[d] = ctx.mark_last("tensor", "pe")
        ctx.wait("vector", "pe", stf[DCH - 1])
        ctx.copy(rowT(0), PSA[0:1, 0:T])
        dv = ctx.copy(rowT(1), PSB[0:1, 0:T])
        ctx.wait("vector", "dve", dv)
        ln_rows_and_bcast(D)
        ctx.barrier()
        pool.free("rows")
        pool.free("sq2")

        pool.alloc("o1", 2 * T * 4)
        pool.alloc("yb", 2 * T * 4)
        y_st, y_dve = {}, {}
        for j in range(DCH):
            p = j % 2
            if j >= 2:
                ctx.wait("vector", "sd", y_st[j - 2])
                ctx.wait("scalar", "dve", y_dve[j - 2])
            sd_m = ctx.dma("scalar", vf32("o1", T, p * T),
                           sOUT1[j * 128:(j + 1) * 128, :], "ad")
            ctx.fence("ad")
            t1 = ctx.tt(vf32("yb", T, p * T), outv(j), vf32("Br", T), ALU.mult)
            ctx.wait("vector", "dve", t1)
            t2 = ctx.tt(vf32("yb", T, p * T), vf32("yb", T, p * T), vf32("Bm", T),
                        ALU.subtract)
            ctx.wait("vector", "ad", sd_m)
            ctx.wait("vector", "dve", t2)
            t3 = ctx.tt(vf32("yb", T, p * T), vf32("yb", T, p * T), vf32("o1", T, p * T),
                        ALU.add)
            y_dve[j] = t3
            ctx.wait("sync", "dve", t3)
            y_st[j] = ctx.dma("sync", yfm[j * 128:(j + 1) * 128, :], vf32("yb", T, p * T),
                              "sd")
            ctx.fence("sd")
        ctx.fence("sd")
        ctx.fence("ad")
        ctx.fence("gd")
        ctx.wait("sync", "sd", ctx.n["sd"])
        ctx.wait("sync", "gd", ctx.n["gd"])
        ctx.wait("sync", "ad", ctx.n["ad"])

        block = E(nc.Block())

        @block.sync
        def _(eng):
            for fn in ctx.progs["sync"]:
                fn(eng)

        @block.gpsimd
        def _(eng):
            for fn in ctx.progs["gpsimd"]:
                fn(eng)

        @block.vector
        def _(eng):
            for fn in ctx.progs["vector"]:
                fn(eng)

        @block.scalar
        def _(eng):
            for fn in ctx.progs["scalar"]:
                fn(eng)

        @block.tensor
        def _(eng):
            for fn in ctx.progs["tensor"]:
                fn(eng)

    return nc


def make_core_inputs(c: CFG, inp, nseq_chunks=4):
    D, T, NT = c.D, c.T, c.NT
    x = np.ascontiguousarray(np.asarray(inp["L0_attn_x"], dtype=np.float32)).reshape(-1, D)
    assert x.shape[0] == NC * T
    def tile_w(a):
        a = np.asarray(a, np.float32)
        Kd, Jd = a.shape
        return np.ascontiguousarray(
            a.reshape(Kd // 128, 128, Jd // 128, 128).transpose(2, 1, 0, 3)
            .reshape(Jd // 128, 128, Kd))

    shared = {
        "w_left": tile_w(inp["L0_attn_left_W"]),
        "w_right": tile_w(inp["L0_attn_right_W"]),
        "w_ga": tile_w(inp["L0_attn_ga_W"]),
        "w_gx": tile_w(inp["L0_attn_gx_W"]),
        "w_out": tile_w(inp["L0_attn_out_W"]),
        "w_up1": tile_w(inp["L0_attn_up1_W"]),
        "w_up2": tile_w(inp["L0_attn_up2_W"]),
        "w_fup": tile_w(np.asarray(inp["L0_ffn_up_weight"], np.float32).T),
        "w_down": np.ascontiguousarray(np.asarray(inp["L0_attn_down_W"], np.float32)),
        "w_fdn": np.ascontiguousarray(np.asarray(inp["L0_ffn_down_weight"], np.float32).T),
        "r_n1w": np.asarray(inp["L0_attn_norm1_w"], np.float32).reshape(1, -1),
        "r_leftb": np.asarray(inp["L0_attn_left_b"], np.float32).reshape(1, -1),
        "r_rightb": np.asarray(inp["L0_attn_right_b"], np.float32).reshape(1, -1),
        "r_convb": np.asarray(inp["L0_attn_conv_b"], np.float32).reshape(1, -1),
        "r_convw": np.ascontiguousarray(np.asarray(inp["L0_attn_conv_w"], np.float32)),
        "r_gab": np.asarray(inp["L0_attn_ga_b"], np.float32).reshape(1, -1),
        "r_gxb": np.asarray(inp["L0_attn_gx_b"], np.float32).reshape(1, -1),
        "r_apar": np.asarray(inp["L0_attn_a_param"], np.float32).reshape(1, -1),
        "r_outb": np.asarray(inp["L0_attn_out_b"], np.float32).reshape(1, -1),
        "r_n2w": np.asarray(inp["L0_attn_norm2_w"], np.float32).reshape(1, -1),
        "r_up1b": np.asarray(inp["L0_attn_up1_b"], np.float32).reshape(1, -1),
        "r_up2b": np.asarray(inp["L0_attn_up2_b"], np.float32).reshape(1, -1),
        "r_downb": np.asarray(inp["L0_attn_down_b"], np.float32).reshape(1, -1),
        "r_uscale": np.asarray(inp["L0_ffn_up_scale"], np.float32).reshape(1, -1),
        "r_dscale": np.asarray(inp["L0_ffn_down_scale"], np.float32).reshape(1, -1),
        "r_oscale": np.asarray(inp["L0_ffn_output_scale"], np.float32).reshape(1, -1),
    }
    maps = []
    for core in range(NC):
        t0 = core * T
        j = core % nseq_chunks
        halo = np.zeros((NT, D), np.float32)
        if j != 0:
            halo[:] = x[t0 - NT:t0]
        blk = np.concatenate([halo, x[t0:t0 + T]], 0)
        m = dict(shared)
        m["xfm"] = np.ascontiguousarray(blk.T)
        m["halo_mask"] = np.full((128, 1), 0.0 if j == 0 else 1.0, np.float32)
        mg = np.zeros(NC, np.float32)
        lg = np.zeros(NC, np.float32)
        s0 = (core // nseq_chunks) * nseq_chunks
        mg[s0:core] = 1.0
        if j > 0:
            lg[core - 1] = 1.0
        m["scanm"] = np.ascontiguousarray(np.broadcast_to(mg, (128, NC)))
        m["scanl"] = np.ascontiguousarray(np.broadcast_to(lg, (128, NC)))
        maps.append(m)
    return maps


def kernel(**inputs):
    c = CFG_FULL
    nc = build_nc(c, debug=False)
    maps = make_core_inputs(c, inputs)
    res = run_bass_kernel_spmd(nc, maps, core_ids=list(range(NC)))
    outs = [np.asarray(res.results[i]["yfm"]).T for i in range(NC)]
    y = np.concatenate(outs, 0).reshape(2, NC * c.T // 2, c.D)
    return np.ascontiguousarray(y.astype(np.float32))


# version 20 (final)
# speedup vs baseline: 1.0199x; 1.0199x over previous
import numpy as np
import concourse.bass as bass
import concourse.mybir as mybir
from concourse.bass_utils import run_bass_kernel_spmd

F32 = mybir.dt.float32
BF16 = mybir.dt.bfloat16
AF = mybir.ActivationFunctionType
ALU = mybir.AluOpType

NC = 8


class CFG:
    def __init__(self, DCH, HCH, FCH, GCH, T, NT, FGRP, HGRP):
        self.DCH, self.HCH, self.FCH, self.GCH = DCH, HCH, FCH, GCH
        self.D, self.H, self.F, self.G = 128 * DCH, 128 * HCH, 128 * FCH, 128 * GCH
        self.T, self.NT = T, NT
        assert T % NT == 0
        self.TH = T + NT
        self.NWR = T // NT
        self.NWH = self.TH // NT
        self.FGRP, self.HGRP = FGRP, HGRP
        assert FCH % FGRP == 0 and GCH % HGRP == 0
        self.THALF = T // 2 if (T // 2) % NT == 0 else T
        assert self.THALF % NT == 0


CFG_FULL = CFG(DCH=16, HCH=16, FCH=48, GCH=64, T=2048, NT=512, FGRP=4, HGRP=4)


def make_small_cfg():
    return CFG(DCH=2, HCH=2, FCH=3, GCH=4, T=512, NT=512, FGRP=3, HGRP=4)


class Pool:

    def __init__(self, nbytes):
        self.nbytes = nbytes
        self.blocks = {}
        self.free_list = [(0, nbytes)]
        self.peak = 0

    def alloc(self, name, size, top=False):
        size = (size + 511) & ~511
        assert name not in self.blocks, name
        idxs = range(len(self.free_list))
        for i in (reversed(idxs) if top else idxs):
            s, sz = self.free_list[i]
            if sz >= size:
                if top:
                    self.blocks[name] = (s + sz - size, size)
                    self.free_list[i] = (s, sz - size)
                    if sz == size:
                        self.free_list.pop(i)
                    return s + sz - size
                self.blocks[name] = (s, size)
                if sz == size:
                    self.free_list.pop(i)
                else:
                    self.free_list[i] = (s + size, sz - size)
                return s
        raise RuntimeError(
            f"pool OOM for {name} ({size}B): blocks={self.blocks} free={self.free_list}")

    def free(self, name):
        s, size = self.blocks.pop(name)
        self.free_list.append((s, size))
        self.free_list.sort()
        merged = []
        for b in self.free_list:
            if merged and merged[-1][0] + merged[-1][1] == b[0]:
                merged[-1] = (merged[-1][0], merged[-1][1] + b[1])
            else:
                merged.append(b)
        self.free_list = merged

    def off(self, name):
        return self.blocks[name][0]


class Ctx:
    ENGS = ("sync", "gpsimd", "vector", "scalar", "tensor")
    MKEY = {"vector": "dve", "scalar": "act", "tensor": "pe",
            "gpsimd": "pool"}

    def __init__(self, nc, sems):
        self.nc = nc
        self.progs = {e: [] for e in self.ENGS}
        self.sem = sems
        self.n = {k: 0 for k in sems}
        self.w = {}
        self.last = {e: None for e in self.ENGS}
        self.fences = {"sd": [0], "gd": [0], "ad": [0]}
        self.qeng = {"sd": "sync", "gd": "gpsimd", "ad": "scalar"}

    def _append(self, eng, fn, markable):
        self.progs[eng].append(fn)
        if markable:
            self.last[eng] = [len(self.progs[eng]) - 1, fn, False]

    def op(self, eng, fn, key=None):
        if key is None:
            self._append(eng, fn, True)
            return None
        self.n[key] += 1
        sem = self.sem[key]
        self._append(eng, (lambda e, f=fn, s=sem: f(e).then_inc(s, 1)), True)
        self.last[eng][2] = True
        return self.n[key]

    def mark_last(self, eng, key):
        rec = self.last[eng]
        assert rec is not None, f"no instruction to mark on {eng}"
        if rec[2]:
            return self.n[key]
        idx, fn, _ = rec
        self.n[key] += 1
        sem = self.sem[key]
        self.progs[eng][idx] = (lambda e, f=fn, s=sem: f(e).then_inc(s, 1))
        rec[2] = True
        return self.n[key]

    def fence(self, key):
        v = self.n[key]
        if v <= self.fences[key][-1]:
            return v
        eng = self.qeng[key]
        sem = self.sem[key]
        self._append(eng, (lambda e, s=sem, vv=v: e.wait_ge(s, vv)), False)
        self.w[(eng, key)] = max(self.w.get((eng, key), 0), v)
        self.fences[key].append(v)
        return v

    def wait(self, eng, key, val):
        if val is None or val <= 0:
            return
        assert val <= self.n[key], (eng, key, val, self.n[key])
        if key in self.fences:
            cand = [f for f in self.fences[key] if f >= val]
            assert cand, (f"no fence covering {key}>={val} (fences "
                          f"{self.fences[key][-3:]}, n={self.n[key]})")
            val = cand[0]
        if val <= self.w.get((eng, key), 0):
            return
        self.w[(eng, key)] = val
        sem = self.sem[key]
        self._append(eng, (lambda e, s=sem, v=val: e.wait_ge(s, v)), False)

    def dma(self, q, out, in_, key, **kw):
        self.n[key] += 16
        sem = self.sem[key]
        self._append(
            q, (lambda e, o=out, i=in_, s=sem, kw=kw:
                e.dma_start(o, i, **kw).then_inc(s, 16)), False)
        return self.n[key]

    def act(self, out, in_, func, bias=0.0, scale=1.0, key="act"):
        return self.op("scalar",
                       lambda e, o=out, i=in_, f=func, b=bias, s=scale:
                       e.activation(o, i, f, bias=b, scale=s), key)

    def tt(self, out, a, b, alu, key="dve"):
        return self.op("vector",
                       lambda e, o=out, x=a, y=b, al=alu: e.tensor_tensor(o, x, y, al),
                       key)

    def ts(self, out, a, s1, s2, op0, op1=None, key="dve"):
        if op1 is None:
            return self.op("vector",
                           lambda e, o=out, x=a, u=s1, al=op0:
                           e.tensor_single_scalar(o, x, u, al), key)
        return self.op("vector",
                       lambda e, o=out, x=a, u=s1, v=s2, a0=op0, a1=op1:
                       e.tensor_scalar(o, x, u, v, a0, a1), key)

    def stt(self, out, in0, scalar, in1, op0, op1, key="dve"):
        return self.op("vector",
                       lambda e, o=out, x=in0, s=scalar, y=in1, a0=op0, a1=op1:
                       e.scalar_tensor_tensor(o, x, s, y, a0, a1), key)

    def scan(self, out, d0, d1, init, op0, op1, key="dve"):
        return self.op("vector",
                       lambda e, o=out, a=d0, b=d1, i=init, a0=op0, a1=op1:
                       e.tensor_tensor_scan(o, a, b, i, a0, a1), key)

    def copy(self, out, in_, key="dve"):
        return self.op("vector", lambda e, o=out, i=in_: e.tensor_copy(o, i), key)

    def memset(self, ap, v, key="dve"):
        return self.op("vector", lambda e, a=ap, vv=v: e.memset(a, vv), key)

    def recip(self, out, in_, key="dve"):
        return self.op("vector", lambda e, o=out, i=in_: e.reciprocal(o, i), key)

    def ptt(self, out, a, b, alu, key="pool"):
        return self.op("gpsimd",
                       lambda e, o=out, x=a, y=b, al=alu: e.tensor_tensor(o, x, y, al),
                       key)

    def mm(self, out, lhsT, rhs, start, stop, key=None, tile_position=None):
        return self.op("tensor",
                       lambda e, o=out, l=lhsT, r=rhs, s0=start, s1=stop, tp=tile_position:
                       e.matmul(o, l, r, start=s0, stop=s1, tile_position=tp), key)

    def barrier(self):
        self.fence("sd")
        self.fence("gd")
        self.fence("ad")
        for eng in self.ENGS:
            k = self.MKEY.get(eng)
            if k is not None and self.last[eng] is not None and not self.last[eng][2]:
                self.mark_last(eng, k)
        for eng in self.ENGS:
            for k, v in self.n.items():
                if v > 0:
                    self.wait(eng, k, v)


def build_nc(c: CFG, debug=False):
    nc = bass.Bass("TRN2", target_bir_lowering=False)
    D, H, F, G, T, NT, TH = c.D, c.H, c.F, c.G, c.T, c.NT, c.TH
    DCH, HCH, FCH, GCH, NWR, NWH = c.DCH, c.HCH, c.FCH, c.GCH, c.NWR, c.NWH
    assert D == H, "layout assumes d_model == mixer width"

    xfm = nc.dram_tensor("xfm", [D, TH], F32, kind="ExternalInput").ap()
    wT = {}
    for nm, jch, kch in (("w_left", HCH, DCH), ("w_right", HCH, DCH),
                         ("w_ga", HCH, HCH), ("w_gx", HCH, HCH),
                         ("w_out", DCH, HCH), ("w_up1", FCH, DCH),
                         ("w_up2", FCH, DCH), ("w_fup", GCH, DCH)):
        wT[nm] = nc.dram_tensor(nm, [jch, 128, kch * 128], F32,
                                kind="ExternalInput").ap()
    for nm, shp in (("w_down", [F, D]), ("w_fdn", [G, D])):
        wT[nm] = nc.dram_tensor(nm, shp, F32, kind="ExternalInput").ap()
    rows_in = {}
    for nm, n in (("r_n1w", D), ("r_leftb", H), ("r_rightb", H), ("r_convb", H),
                  ("r_gab", H), ("r_gxb", H), ("r_apar", H), ("r_outb", D),
                  ("r_n2w", D), ("r_up1b", F), ("r_up2b", F), ("r_downb", D),
                  ("r_uscale", G), ("r_dscale", D), ("r_oscale", 1)):
        rows_in[nm] = nc.dram_tensor(nm, [1, n], F32, kind="ExternalInput").ap()
    rows_in["r_convw"] = nc.dram_tensor("r_convw", [4, H], F32, kind="ExternalInput").ap()
    halo_mask = nc.dram_tensor("halo_mask", [128, 1], F32, kind="ExternalInput").ap()
    scanm_in = nc.dram_tensor("scanm", [128, NC], F32, kind="ExternalInput").ap()
    scanl_in = nc.dram_tensor("scanl", [128, NC], F32, kind="ExternalInput").ap()

    yfm = nc.dram_tensor("yfm", [D, T], F32, kind="ExternalOutput").ap()

    skind = "ExternalOutput" if debug else "Internal"
    sLEFT = nc.dram_tensor("sLEFT", [H, T], BF16, kind=skind).ap()
    sHL = nc.dram_tensor("sHL", [H, T], BF16, kind=skind).ap()
    sAC = nc.dram_tensor("sAC", [H, T], BF16, kind=skind).ap()
    sX1 = nc.dram_tensor("sX1", [D, T], F32, kind=skind).ap()
    sG = nc.dram_tensor("sG", [F, T], BF16, kind=skind).ap()
    sBO = nc.dram_tensor("sBO", [D, T], F32, kind=skind).ap()
    sOUT1 = nc.dram_tensor("sOUT1", [D, T], F32, kind=skind).ap()
    sHID = nc.dram_tensor("sHID", [G, T], BF16, kind=skind).ap()
    if debug:
        dXN = nc.dram_tensor("dXN", [D, TH], BF16, kind="ExternalOutput").ap()
        dGT = nc.dram_tensor("dGT", [H, T], BF16, kind="ExternalOutput").ap()
        dX1N = nc.dram_tensor("dX1N", [D, T], BF16, kind="ExternalOutput").ap()
        dU2 = nc.dram_tensor("dU2", [D, T], BF16, kind="ExternalOutput").ap()
        dHIN = nc.dram_tensor("dHIN", [128, HCH], F32, kind="ExternalOutput").ap()
    cc_in = nc.dram_tensor("cc_in", [2, 128, HCH], F32)
    cc_out = nc.dram_tensor("cc_out", [NC, 2, 128, HCH], F32)

    from contextlib import ExitStack
    with ExitStack() as es:
        E = es.enter_context
        sems = {k: E(nc.semaphore(k)) for k in ("sd", "gd", "ad", "pe", "dve", "act",
                                                "cc", "pool")}
        ctx = Ctx(nc, sems)

        onesf = E(nc.sbuf_tensor("onesf", [128, 1], F32))
        onesb = E(nc.sbuf_tensor("onesb", [128, 1], BF16))
        ones1f = E(nc.sbuf_tensor("ones1f", [1, 128], F32))
        cEPS5 = E(nc.sbuf_tensor("cEPS5", [128, 1], F32))
        cEPS8 = E(nc.sbuf_tensor("cEPS8", [128, 1], F32))
        cONE = E(nc.sbuf_tensor("cONE", [128, 1], F32))
        hm = E(nc.sbuf_tensor("hm", [128, 1], F32))
        scanm = E(nc.sbuf_tensor("scanm_t", [128, NC], F32))
        scanl = E(nc.sbuf_tensor("scanl_t", [128, NC], F32))
        DVECS = ["r_n1w", "r_leftb", "r_rightb", "r_gab", "r_gxb", "r_apar",
                 "r_outb", "r_n2w", "r_downb", "r_dscale", "r_convb",
                 "cw0", "cw1", "cw2", "cw3"]
        NV = len(DVECS)
        bias_d = E(nc.sbuf_tensor("bias_d", [128, DCH * NV], F32))
        bias_f = E(nc.sbuf_tensor("bias_f", [128, FCH * 2], F32))
        bias_g = E(nc.sbuf_tensor("bias_g", [128, GCH], F32))
        sa_t = E(nc.sbuf_tensor("sa_t", [128, HCH], F32))
        osc = E(nc.sbuf_tensor("osc", [128, 1], F32))
        osrow = E(nc.sbuf_tensor("osrow", [1, 1], F32))
        dsc2 = E(nc.sbuf_tensor("dsc2", [128, DCH], F32))
        acol = E(nc.sbuf_tensor("acol", [128, HCH], F32))
        ecol = E(nc.sbuf_tensor("ecol", [128, HCH], F32))
        agat = E(nc.sbuf_tensor("agat", [128, NC * 2 * HCH], F32))
        hin = E(nc.sbuf_tensor("hin", [128, HCH], F32))
        ctile = E(nc.sbuf_tensor("ctile", [128, HCH], F32))

        PSA = E(nc.psum_tensor("PSA", [128, 2048], F32))
        PSB = E(nc.psum_tensor("PSB", [128, 2048], F32))

        rem = nc.sbuf_bytes_remaining
        pool_bytes = (rem - 2048) & ~1023
        POOL = E(nc.sbuf_tensor("POOL", [128, pool_bytes // 2], BF16))
        pool_f32 = POOL.ap().bitcast(F32)
        pool_bf = POOL.ap()
        pool = Pool(pool_bytes)

        def vf32(name, cols, coloff=0):
            o = pool.off(name) // 4
            return pool_f32[:, o + coloff: o + coloff + cols]

        def vbf(name, cols, coloff=0):
            o = pool.off(name) // 2
            return pool_bf[:, o + coloff: o + coloff + cols]

        def psw(i, rows=None):
            t = PSA if (i * NT) < 2048 else PSB
            col = (i * NT) % 2048
            if rows is None:
                return t[:, col:col + NT]
            return t[rows[0]:rows[1], col:col + NT]

        ctx.memset(onesf[:], 1.0)
        ctx.memset(onesb[:], 1.0)
        ctx.memset(ones1f[:], 1.0)
        ctx.memset(cEPS5[:], 1e-5)
        ctx.memset(cEPS8[:], 1e-8)
        ctx.memset(cONE[:], 1.0)
        ctx.dma("sync", hm[:], halo_mask, "sd")
        ctx.dma("sync", scanm[:], scanm_in, "sd")
        ctx.dma("sync", scanl[:], scanl_in, "sd")
        ctx.dma("sync", osrow[:], rows_in["r_oscale"], "sd")

        ones11 = E(nc.sbuf_tensor("ones11", [1, 1], F32))
        m11 = ctx.memset(ones11[:], 1.0)
        ctx.wait("tensor", "dve", m11)
        stage_sz = max(15 * D, 2 * F, G) * 4
        pool.alloc("stage", stage_sz)

        def stg(p, n):
            return vf32("stage", n, 0)[0:1, :]

        def row_src(nm):
            if nm.startswith("cw"):
                k = int(nm[2])
                return rows_in["r_convw"][k:k + 1, :]
            return rows_in[nm]

        def colize(vecs, dest, ch):
            nvec = len(vecs)
            offs = []
            o = 0
            for nm in vecs:
                offs.append(o)
                o += 128 * ch
            sm = 0
            ctx.wait("scalar", "pe", ctx.n["pe"])
            for vi, nm in enumerate(vecs):
                sm = ctx.dma("scalar", stg(0, o)[0:1, offs[vi]:offs[vi] + 128 * ch],
                             row_src(nm), "ad")
            ctx.fence("ad")
            ctx.wait("tensor", "ad", sm)
            ctx.wait("tensor", "dve", ctx.n["dve"])
            pms = {}
            for vi in range(nvec):
                t = (PSA, PSB)[vi % 2]
                for j in range(ch):
                    ctx.mm(t[:, (vi // 2) * ch + j:(vi // 2) * ch + j + 1],
                           stg(0, o)[0:1, offs[vi] + j * 128:offs[vi] + (j + 1) * 128],
                           ones11[:], start=True, stop=True)
                pms[vi] = ctx.mark_last("tensor", "pe")
            for vi in range(nvec):
                t = (PSA, PSB)[vi % 2]
                ctx.wait("vector", "pe", pms[vi])
                ctx.copy(dest[:, vi * ch:(vi + 1) * ch],
                         t[:, (vi // 2) * ch:(vi // 2) * ch + ch])
            ctx.wait("tensor", "dve", ctx.mark_last("vector", "dve"))
        colize(DVECS, bias_d, DCH)
        colize(["r_up1b", "r_up2b"], bias_f, FCH)
        colize(["r_uscale"], bias_g, GCH)
        pool.free("stage")
        gd_bias = ctx.fence("gd")
        ctx.fence("sd")

        def bd(nm, j):
            v = DVECS.index(nm)
            return bias_d[:, v * DCH + j: v * DCH + j + 1]
        ctx.barrier()

        pool.alloc("xb", 2 * TH * 4)
        pool.alloc("sq", 2 * TH * 2)
        pool.alloc("rows", 4 * TH * 4)
        pool.alloc("B0", T * 4, top=True)
        pool.alloc("B1", T * 4, top=True)
        pool.alloc("B2", TH * 4)
        pool.alloc("B3", TH * 4)

        def xbv(p):
            return vf32("xb", TH, p * TH)

        def sqv(p):
            return vbf("sq", TH, p * TH)

        def rowv(i, cols=None, co=0):
            return vf32("rows", cols if cols is not None else TH, i * TH + co)[0:1, :]

        sq_m, st_m = {}, {}
        for j in range(DCH):
            p = j % 2
            q, qk = (("sync", "sd") if p == 0 else ("gpsimd", "gd"))
            if j >= 2:
                ctx.wait(q, "act", sq_m[j - 2])
                ctx.wait(q, "pe", st_m[j - 2])
                ctx.wait("scalar", "pe", st_m[j - 2])
            xb_m = ctx.dma(q, xbv(p), xfm[j * 128:(j + 1) * 128, :], qk)
            ctx.fence(qk)
            ctx.wait("scalar", qk, xb_m)
            sq_m[j] = ctx.act(sqv(p)[:, 0:T], xbv(p)[:, NT:TH], AF.Square)
            ctx.wait("tensor", "act", sq_m[j])
            ctx.wait("tensor", qk, xb_m)
            for w in range(NWR):
                ctx.mm(PSA[0:1, w * NT:(w + 1) * NT], onesf[:],
                       xbv(p)[:, NT + w * NT:NT + (w + 1) * NT],
                       start=(j == 0), stop=(j == DCH - 1))
                ctx.mm(PSB[0:1, w * NT:(w + 1) * NT], onesb[:],
                       sqv(p)[:, w * NT:(w + 1) * NT],
                       start=(j == 0), stop=(j == DCH - 1))
            st_m[j] = ctx.mark_last("tensor", "pe")
        ctx.wait("vector", "pe", st_m[DCH - 1])
        for w in range(NWR):
            ctx.copy(rowv(0, NT, NT + w * NT), PSA[0:1, w * NT:(w + 1) * NT])
            ctx.copy(rowv(1, NT, NT + w * NT), PSB[0:1, w * NT:(w + 1) * NT])
        dv0 = ctx.mark_last("vector", "dve")
        pe_main = st_m[DCH - 1]
        sqh_m, sth_m = {}, {}
        for j in range(DCH):
            p = j % 2
            q, qk = (("sync", "sd") if p == 0 else ("gpsimd", "gd"))
            if j < 2:
                ctx.wait(q, "pe", pe_main)
            else:
                ctx.wait(q, "act", sqh_m[j - 2])
                ctx.wait(q, "pe", sth_m[j - 2])
                ctx.wait("scalar", "pe", sth_m[j - 2])
            xb_m = ctx.dma(q, xbv(p)[:, 0:NT], xfm[j * 128:(j + 1) * 128, 0:NT], qk)
            ctx.fence(qk)
            ctx.wait("scalar", qk, xb_m)
            sqh_m[j] = ctx.act(sqv(p)[:, 0:NT], xbv(p)[:, 0:NT], AF.Square)
            ctx.wait("tensor", "act", sqh_m[j])
            if j == 0:
                ctx.wait("tensor", "dve", dv0)
            ctx.mm(PSA[0:1, 0:NT], onesf[:], xbv(p)[:, 0:NT],
                   start=(j == 0), stop=(j == DCH - 1))
            ctx.mm(PSB[0:1, 0:NT], onesb[:], sqv(p)[:, 0:NT],
                   start=(j == 0), stop=(j == DCH - 1))
            sth_m[j] = ctx.mark_last("tensor", "pe")
        ctx.wait("vector", "pe", sth_m[DCH - 1])
        ctx.copy(rowv(0, NT, 0), PSA[0:1, 0:NT])
        ctx.copy(rowv(1, NT, 0), PSB[0:1, 0:NT])
        dv = ctx.mark_last("vector", "dve")
        ctx.wait("vector", "dve", dv)

        ctx.ts(rowv(0), rowv(0), 1.0 / D, None, ALU.mult)
        d0 = ctx.ts(rowv(1), rowv(1), 1.0 / D, None, ALU.mult)
        ctx.wait("vector", "dve", d0)
        d1 = ctx.tt(rowv(2), rowv(0), rowv(0), ALU.mult)
        ctx.wait("vector", "dve", d1)
        d2 = ctx.tt(rowv(1), rowv(1), rowv(2), ALU.subtract)
        ctx.wait("scalar", "dve", d2)
        a1 = ctx.act(rowv(2), rowv(1), AF.Sqrt, bias=cEPS5[0:1, 0:1])
        ctx.wait("vector", "act", a1)
        d3 = ctx.recip(rowv(2), rowv(2))
        ctx.wait("vector", "dve", d3)
        d4 = ctx.tt(rowv(3), rowv(1), rowv(2), ALU.mult)
        ctx.wait("vector", "dve", d4)
        d5 = ctx.tt(rowv(3), rowv(3), rowv(2), ALU.mult)
        ctx.wait("scalar", "dve", d5)
        a2 = ctx.act(rowv(3), rowv(3), AF.Sqrt, bias=cEPS8[0:1, 0:1])
        ctx.wait("vector", "act", a2)
        d6 = ctx.recip(rowv(3), rowv(3))
        ctx.wait("vector", "dve", d6)
        d7 = ctx.tt(rowv(3), rowv(3), rowv(2), ALU.mult)
        ctx.wait("vector", "dve", d7)
        ctx.wait("tensor", "dve", d7)

        def bcast(row_ap, nwin, dest_fn):
            cps = {}
            ctx.wait("tensor", "dve", ctx.n["dve"])
            for w in range(nwin):
                t = PSA if (w % 2 == 0) else PSB
                if w >= 2:
                    ctx.wait("tensor", "dve", cps[w - 2])
                pm = ctx.mm(t[:, 1536:1536 + NT], ones1f[:],
                            row_ap[0:1, w * NT:(w + 1) * NT], start=True, stop=True,
                            key="pe")
                ctx.wait("vector", "pe", pm)
                cps[w] = ctx.copy(dest_fn(w), t[:, 1536:1536 + NT])

        bcast(rowv(3), NWH, lambda w: vf32("B2", NT, w * NT))
        d8 = ctx.tt(rowv(3), rowv(3), rowv(0), ALU.mult)
        ctx.wait("vector", "dve", d8)
        ctx.wait("tensor", "dve", d8)
        bcast(rowv(3), NWH, lambda w: vf32("B3", NT, w * NT))
        bcast(rowv(2)[0:1, NT:TH], NWR, lambda w: vf32("B0", NT, w * NT))
        d9 = ctx.tt(rowv(2), rowv(2), rowv(0), ALU.mult)
        ctx.wait("vector", "dve", d9)
        ctx.wait("tensor", "dve", d9)
        bcast(rowv(2)[0:1, NT:TH], NWR, lambda w: vf32("B1", NT, w * NT))

        ctx.wait("vector", "gd", gd_bias)
        vap = DVECS.index("r_apar")
        cp_sa = ctx.copy(sa_t[:, 0:HCH], bias_d[:, vap * DCH:vap * DCH + HCH])
        dvv = ctx.mark_last("vector", "dve")
        ctx.wait("scalar", "dve", dvv)
        aa1 = ctx.act(ctile[:, 0:HCH], sa_t[:, 0:HCH], AF.Abs)
        ctx.wait("scalar", "act", aa1)
        aa2 = ctx.act(ctile[:, 0:HCH], ctile[:, 0:HCH], AF.Exp, scale=-1.0)
        ctx.wait("scalar", "act", aa2)
        aa3 = ctx.act(ctile[:, 0:HCH], ctile[:, 0:HCH], AF.Ln, bias=cONE[:, 0:1])
        ctx.wait("vector", "act", aa3)
        dd1 = ctx.ts(sa_t[:, 0:HCH], sa_t[:, 0:HCH], 0.0, None, ALU.max)
        ctx.wait("vector", "dve", dd1)
        dd2 = ctx.tt(sa_t[:, 0:HCH], sa_t[:, 0:HCH], ctile[:, 0:HCH], ALU.add)
        ctx.wait("vector", "dve", dd2)
        ctx.ts(sa_t[:, 0:HCH], sa_t[:, 0:HCH], -8.0, None, ALU.mult)
        ctx.wait("vector", "sd", ctx.n["sd"])
        pm = ctx.mm(PSB[:, 8:9], ones1f[:], osrow[0:1, 0:1], start=True, stop=True,
                    key="pe")
        ctx.wait("vector", "pe", pm)
        ctx.copy(osc[:], PSB[:, 8:9])
        vds = DVECS.index("r_dscale")
        ctx.copy(dsc2[:, 0:DCH], bias_d[:, vds * DCH:vds * DCH + DCH])
        dvv = ctx.mark_last("vector", "dve")
        ctx.wait("vector", "dve", dvv)
        ctx.ts(dsc2[:, 0:DCH], dsc2[:, 0:DCH], osc[:, 0:1], None, ALU.mult)
        ctx.barrier()
        pool.free("rows")
        pool.free("sq")

        pool.alloc("XN", DCH * TH * 2, top=True)

        def xnv(k, w0=0, cols=None):
            return vbf("XN", cols if cols is not None else TH, k * TH + w0)

        e_m = {}
        for j in range(DCH):
            p = j % 2
            q, qk = (("sync", "sd") if p == 0 else ("gpsimd", "gd"))
            if j >= 2:
                ctx.wait(q, "act", e_m[j - 2])
            xb_m = ctx.dma(q, xbv(p), xfm[j * 128:(j + 1) * 128, :], qk)
            ctx.fence(qk)
            ctx.wait("vector", qk, xb_m)
            t_m = ctx.tt(xbv(p), xbv(p), vf32("B2", TH), ALU.mult)
            ctx.wait("vector", "dve", t_m)
            s_m = ctx.tt(xbv(p), xbv(p), vf32("B3", TH), ALU.subtract)
            ctx.wait("scalar", "dve", s_m)
            e_m[j] = ctx.act(xnv(j), xbv(p), AF.Copy, scale=bd("r_n1w", j))
        ctx.barrier()
        pool.free("B2")
        pool.free("B3")
        pool.free("xb")
        if debug:
            for j in range(DCH):
                ctx.dma("sync", dXN[j * 128:(j + 1) * 128, :], xnv(j), "sd")
            ctx.barrier()

        pool.alloc("rp", TH * 4)
        pool.alloc("cvt", T * 4)
        pool.alloc("wc", 2 * DCH * 128 * 2)
        pool.alloc("CV", HCH * T * 2, top=True)

        def cvv(k, w0=0, cols=None):
            return vbf("CV", cols if cols is not None else T, k * T + w0)

        def wcv(p, k):
            return vbf("wc", 128, p * DCH * 128 + k * 128)

        def load_wcol(wap, j, p, Kch, slot="wc", wf_slot=0, pe_gate=None):
            if pe_gate:
                ctx.wait("gpsimd", "pe", pe_gate)
            ob = pool.off(slot) // 2 + p * Kch * 128
            v = ctx.dma("gpsimd", pool_bf[:, ob:ob + Kch * 128], wap[j], "gd")
            ctx.fence("gd")
            return v

        rp_v = vf32("rp", TH)
        cvt_v = vf32("cvt", T)
        ev, cv_m, pe_m = {}, {}, {}
        for j in range(HCH):
            p = j % 2
            g_m = load_wcol(wT["w_right"], j, p, DCH, "wc", 0,
                            pe_gate=pe_m.get(j - 2))
            ctx.wait("tensor", "gd", g_m)
            if j >= 1:
                ctx.wait("tensor", "act", ev[j - 1])
            for k in range(DCH):
                for w in range(NWH):
                    ctx.mm(psw(w), wcv(p, k), xnv(k, w * NT, NT),
                           start=(k == 0), stop=(k == DCH - 1))
            pe_m[j] = ctx.mark_last("tensor", "pe")
            ctx.wait("scalar", "pe", pe_m[j])
            if j >= 1:
                ctx.wait("scalar", "dve", cv_m[j - 1])
            for w in range(NWH):
                ev[j] = ctx.act(rp_v[:, w * NT:(w + 1) * NT], psw(w), AF.Identity,
                                bias=bd("r_rightb", j))
            ctx.wait("vector", "act", ev[j])
            hm_m = ctx.ts(rp_v[:, NT - 3:NT], rp_v[:, NT - 3:NT], hm[:, 0:1], None,
                          ALU.mult)
            ctx.wait("scalar", "act", ev[j])
            a_m = ctx.act(cvt_v[:], rp_v[:, NT:TH], AF.Identity, scale=bd("cw3", j),
                          bias=bd("r_convb", j))
            ctx.wait("vector", "act", a_m)
            ctx.wait("vector", "dve", hm_m)
            s1 = ctx.stt(cvt_v[:], rp_v[:, NT - 1:TH - 1], bd("cw2", j), cvt_v[:],
                         ALU.mult, ALU.add)
            ctx.wait("vector", "dve", s1)
            s2 = ctx.stt(cvt_v[:], rp_v[:, NT - 2:TH - 2], bd("cw1", j), cvt_v[:],
                         ALU.mult, ALU.add)
            ctx.wait("vector", "dve", s2)
            cv_m[j] = ctx.stt(cvv(j), rp_v[:, NT - 3:TH - 3], bd("cw0", j), cvt_v[:],
                              ALU.mult, ALU.add)
        ctx.barrier()
        pool.free("rp")
        pool.free("cvt")

        pool.alloc("lb", 2 * T * 2)
        lb_st, lev, pel = {}, {}, {}
        for j in range(HCH):
            p = j % 2
            g_m = load_wcol(wT["w_left"], j, p, DCH, "wc", 0, pe_gate=pel.get(j - 2))
            ctx.wait("tensor", "gd", g_m)
            t = (PSA, PSB)[p]
            if j >= 2:
                ctx.wait("tensor", "act", lev[j - 2])
            for k in range(DCH):
                for w in range(NWR):
                    ctx.mm(t[:, w * NT:(w + 1) * NT], wcv(p, k),
                           xnv(k, NT + w * NT, NT), start=(k == 0), stop=(k == DCH - 1))
            pel[j] = ctx.mark_last("tensor", "pe")
            ctx.wait("scalar", "pe", pel[j])
            if j >= 2:
                ctx.wait("scalar", "sd", lb_st[j - 2])
            lev[j] = ctx.act(vbf("lb", T, p * T), t[:, 0:T], AF.Gelu,
                             bias=bd("r_leftb", j))
            ctx.wait("sync", "act", lev[j])
            lb_st[j] = ctx.dma("sync", sLEFT[j * 128:(j + 1) * 128, :],
                               vbf("lb", T, p * T), "sd")
            ctx.fence("sd")
        ctx.barrier()
        pool.free("XN")
        pool.free("lb")
        pool.free("wc")

        pool.alloc("wcga", 2 * HCH * 128 * 2)
        pool.alloc("wcgx", 2 * HCH * 128 * 2)
        for nm in ("e1", "e2", "e3", "e4"):
            pool.alloc(nm, 2 * T * 4)
        for nm in ("e5", "e6", "zeros"):
            pool.alloc(nm, T * 4)
        pool.alloc("e5b", T * 2)
        pool.alloc("e6b", T * 2)
        z_m = ctx.memset(vf32("zeros", T), 0.0)

        def ev_(nm, p):
            return vf32(nm, T, p * T)

        def wcg(slot, p, k):
            return vbf(slot, 128, p * HCH * 128 + k * 128)

        st_hl, sig_i, pega, pegx = {}, {}, {}, {}
        hist = {}
        e5v, e6v = vf32("e5", T), vf32("e6", T)
        for j in range(HCH):
            p = j % 2
            g1 = load_wcol(wT["w_ga"], j, p, HCH, "wcga", 0, pe_gate=pega.get(j - 2))
            g2 = load_wcol(wT["w_gx"], j, p, HCH, "wcgx", 1, pe_gate=pegx.get(j - 2))
            ctx.wait("tensor", "gd", g1)
            if j >= 1:
                ctx.wait("tensor", "act", sig_i[j - 1])
            for k in range(HCH):
                for w in range(NWR):
                    ctx.mm(PSA[:, w * NT:(w + 1) * NT], wcg("wcga", p, k),
                           cvv(k, w * NT, NT), start=(k == 0), stop=(k == HCH - 1))
            pega[j] = ctx.mark_last("tensor", "pe")
            ctx.wait("tensor", "gd", g2)
            for k in range(HCH):
                for w in range(NWR):
                    ctx.mm(PSB[:, w * NT:(w + 1) * NT], wcg("wcgx", p, k),
                           cvv(k, w * NT, NT), start=(k == 0), stop=(k == HCH - 1))
            pegx[j] = ctx.mark_last("tensor", "pe")
            e1, e2, e3, e4 = (ev_("e1", p), ev_("e2", p), ev_("e3", p), ev_("e4", p))
            ctx.wait("scalar", "pe", pega[j])
            if j >= 2:
                ctx.wait("scalar", "dve", hist[("cs", j - 2)])
            r_m = ctx.act(e1, PSA[:, 0:T], AF.Sigmoid, bias=bd("r_gab", j))
            ctx.wait("scalar", "pe", pegx[j])
            if j >= 2:
                ctx.wait("scalar", "dve", hist[("h", j - 2)])
            sig_i[j] = ctx.act(e3, PSB[:, 0:T], AF.Sigmoid, bias=bd("r_gxb", j))
            ctx.wait("scalar", "act", r_m)
            la_m = ctx.act(e1, e1, AF.Copy, scale=sa_t[:, j:j + 1])
            ctx.wait("scalar", "act", la_m)
            a_m = ctx.act(e2, e1, AF.Exp)
            ctx.wait("scalar", "act", a_m)
            if j >= 2:
                ctx.wait("scalar", "dve", hist[("b2", j - 2)])
                ctx.wait("scalar", "act", hist[("mu", j - 2)])
            q_m = ctx.act(e4, e2, AF.Square)
            ctx.wait("vector", "act", q_m)
            q2_m = ctx.ts(e4, e4, 1.0, -1.0, ALU.min, ALU.mult)
            ctx.wait("scalar", "dve", q2_m)
            mu_m = ctx.act(e4, e4, AF.Sqrt, bias=cONE[:, 0:1])
            hist[("mu", j)] = mu_m
            ctx.wait("vector", "act", sig_i[j])
            b1_m = ctx.tt(e3, e3, cvv(j), ALU.mult)
            ctx.wait("vector", "act", mu_m)
            ctx.wait("vector", "dve", b1_m)
            b2_m = ctx.tt(e3, e3, e4, ALU.mult)
            hist[("b2", j)] = b2_m
            ctx.wait("vector", "act", a_m)
            ctx.wait("vector", "dve", b2_m)
            if j >= 1:
                ctx.wait("vector", "dve", hist[("cast", j - 1)])
            h_m = ctx.scan(e5v, e2, e3, 0.0, ALU.mult, ALU.add)
            if j == 0:
                ctx.wait("vector", "dve", z_m)
            ctx.wait("vector", "dve", h_m)
            hist[("h", j)] = h_m
            cs_m = ctx.scan(e6v, e1, vf32("zeros", T), 0.0, ALU.add, ALU.add)
            hist[("cs", j)] = cs_m
            ctx.wait("scalar", "dve", cs_m)
            ctx.wait("scalar", "dve", h_m)
            ac_m = ctx.act(e6v, e6v, AF.Exp)
            ctx.wait("vector", "act", ac_m)
            ctx.copy(acol[:, j:j + 1], e6v[:, T - 1:T])
            ctx.copy(ecol[:, j:j + 1], e5v[:, T - 1:T])
            if j >= 1:
                ctx.wait("vector", "sd", st_hl[j - 1])
            ctx.copy(vbf("e5b", T), e5v)
            cp_m = ctx.copy(vbf("e6b", T), e6v)
            hist[("cast", j)] = cp_m
            ctx.wait("sync", "dve", cp_m)
            ctx.dma("sync", sHL[j * 128:(j + 1) * 128, :], vbf("e5b", T), "sd")
            st_hl[j] = ctx.dma("sync", sAC[j * 128:(j + 1) * 128, :], vbf("e6b", T),
                               "sd")
            ctx.fence("sd")
        ctx.barrier()
        for nm in ("wcga", "wcgx", "e1", "e2", "e3", "e4", "e5", "e6", "zeros",
                   "e5b", "e6b"):
            pool.free(nm)
        pool.free("CV")

        pool.alloc("wc", 2 * HCH * 128 * 2)
        wld6 = {}
        for j6 in (0, 1):
            wld6[j6] = load_wcol(wT["w_out"], j6, j6 % 2, HCH, "wc", 0)
        ctx.dma("sync", cc_in.ap()[0], acol[:, 0:HCH], "sd")
        sd_cc = ctx.dma("sync", cc_in.ap()[1], ecol[:, 0:HCH], "sd")
        ctx.fence("sd")
        pool.alloc("GT", HCH * T * 2, top=True)
        pool.alloc("ldl", 2 * T * 2)
        pool.alloc("ldh", 2 * T * 2)
        pool.alloc("lda", 2 * T * 2)
        pool.alloc("e1", T * 4)
        p5_pre = {}
        for j5 in (0, 1):
            l_m = ctx.dma("sync", vbf("ldl", T, j5 * T),
                          sLEFT[j5 * 128:(j5 + 1) * 128, :], "sd")
            ctx.fence("sd")
            h_m = ctx.dma("scalar", vbf("ldh", T, j5 * T),
                          sHL[j5 * 128:(j5 + 1) * 128, :], "ad")
            ctx.fence("ad")
            a_m = ctx.dma("gpsimd", vbf("lda", T, j5 * T),
                          sAC[j5 * 128:(j5 + 1) * 128, :], "gd")
            ctx.fence("gd")
            p5_pre[j5] = (l_m, h_m, a_m)
        ctx.wait("gpsimd", "sd", sd_cc)
        ctx.op("gpsimd",
               lambda e: e.collective_compute(
                   "AllGather", ALU.bypass, replica_groups=[list(range(NC))],
                   ins=[cc_in.ap().opt()], outs=[cc_out.ap().opt()]), "cc")
        ctx.wait("sync", "cc", 1)
        sd_ag = ctx.dma("sync",
                        agat.ap()[:, 0:NC * 2 * HCH].rearrange(
                            "p (g r c) -> p g r c", g=NC, r=2),
                        cc_out.ap().rearrange("g r p c -> p g r c"), "sd")
        ctx.fence("sd")
        ctx.wait("vector", "sd", sd_ag)
        ctx.memset(ctile[:, 0:HCH], 0.0)
        m1_ = ctx.memset(hin[:, 0:HCH], 0.0)
        ctx.wait("vector", "dve", m1_)
        for g in range(NC):
            Ag = agat[:, (g * 2 + 0) * HCH:(g * 2 + 0) * HCH + HCH]
            Eg = agat[:, (g * 2 + 1) * HCH:(g * 2 + 1) * HCH + HCH]
            t1 = ctx.tt(ctile[:, 0:HCH], ctile[:, 0:HCH], Ag, ALU.mult)
            ctx.wait("vector", "dve", t1)
            t2 = ctx.tt(ctile[:, 0:HCH], ctile[:, 0:HCH], Eg, ALU.add)
            ctx.wait("vector", "dve", t2)
            t3 = ctx.ts(ctile[:, 0:HCH], ctile[:, 0:HCH], scanm[:, g:g + 1], None,
                        ALU.mult)
            ctx.wait("vector", "dve", t3)
            t4 = ctx.stt(hin[:, 0:HCH], ctile[:, 0:HCH], scanl[:, g:g + 1],
                         hin[:, 0:HCH], ALU.mult, ALU.add)
            ctx.wait("vector", "dve", t4)
        if debug:
            ctx.barrier()
            ctx.dma("sync", dHIN, hin[:, 0:HCH], "sd")
        ctx.barrier()

        e1v = vf32("e1", T)

        def gtv(k, w0=0, cols=None):
            return vbf("GT", cols if cols is not None else T, k * T + w0)

        gt_m = {}
        for j in range(HCH):
            p = j % 2
            if j >= 2:
                ctx.wait("sync", "dve", gt_m[j - 2])
                ctx.wait("scalar", "dve", gt_m[j - 2])
                ctx.wait("gpsimd", "dve", gt_m[j - 2])
                l_m = ctx.dma("sync", vbf("ldl", T, p * T),
                              sLEFT[j * 128:(j + 1) * 128, :], "sd")
                ctx.fence("sd")
                h_m = ctx.dma("scalar", vbf("ldh", T, p * T),
                              sHL[j * 128:(j + 1) * 128, :], "ad")
                ctx.fence("ad")
                a_m = ctx.dma("gpsimd", vbf("lda", T, p * T),
                              sAC[j * 128:(j + 1) * 128, :], "gd")
                ctx.fence("gd")
            else:
                l_m, h_m, a_m = p5_pre[j]
            ctx.wait("vector", "ad", h_m)
            ctx.wait("vector", "gd", a_m)
            if j >= 1:
                ctx.wait("vector", "dve", gt_m[j - 1])
            t_m = ctx.stt(e1v, vbf("lda", T, p * T), hin[:, j:j + 1],
                          vbf("ldh", T, p * T), ALU.mult, ALU.add)
            ctx.wait("vector", "sd", l_m)
            ctx.wait("vector", "dve", t_m)
            gt_m[j] = ctx.tt(gtv(j), e1v, vbf("ldl", T, p * T), ALU.mult)
        ctx.barrier()
        for nm in ("ldl", "ldh", "lda", "e1"):
            pool.free(nm)
        if debug:
            for j in range(HCH):
                ctx.dma("sync", dGT[j * 128:(j + 1) * 128, :], gtv(j), "sd")
            ctx.barrier()

        pool.alloc("xb", 2 * TH * 4)
        pool.alloc("e1", T * 4)
        pool.alloc("e2", T * 4)
        pool.alloc("x1b", 2 * T * 4)
        pool.alloc("sqT", 2 * T * 2)
        TH2 = T // 2

        x1_st, ev_m, st6_m, pe6 = {}, {}, {}, {}
        x1_dve, sq6_m = {}, {}

        def p6_sq(qi2):
            if qi2 < 0:
                return
            j2, hf2 = qi2 // 2, qi2 % 2
            h02 = hf2 * TH2
            ctx.wait("scalar", "dve", x1_dve[qi2])
            if qi2 >= 4:
                ctx.wait("scalar", "pe", st6_m[qi2 - 4])
            sq6_m[qi2] = ctx.act(vbf("sqT", TH2, (qi2 % 4) * TH2),
                                 vf32("x1b", TH2, (qi2 % 4) * TH2), AF.Square)
            ctx.wait("tensor", "act", sq6_m[qi2])
            for w in range(2):
                ctx.mm(PSB[0:1, h02 + w * NT:h02 + (w + 1) * NT], onesb[:],
                       vbf("sqT", NT, (qi2 % 4) * TH2 + w * NT), start=(j2 == 0),
                       stop=(j2 == DCH - 1))
            st6_m[qi2] = ctx.mark_last("tensor", "pe")

        for j in range(DCH):
            p = j % 2
            if j >= 2:
                wld6[j] = load_wcol(wT["w_out"], j, p, HCH, "wc", 0,
                                    pe_gate=pe6.get(j - 2))
            xb_m = ctx.dma("sync", xbv(p)[:, 0:T], xfm[j * 128:(j + 1) * 128, NT:TH],
                           "sd")
            ctx.fence("sd")
            ctx.wait("tensor", "gd", wld6[j])
            for hf in range(2):
                qi = j * 2 + hf
                h0 = hf * TH2
                c0 = (qi % 2) * TH2
                if qi >= 2:
                    ctx.wait("tensor", "act", ev_m[qi - 2])
                for k in range(HCH):
                    for w in range(2):
                        ctx.mm(PSA[:, c0 + w * NT:c0 + (w + 1) * NT], wcv(p, k),
                               gtv(k, h0 + w * NT, NT),
                               start=(k == 0), stop=(k == HCH - 1))
                pe6[j] = ctx.mark_last("tensor", "pe")
                p6_sq(qi - 1)
                ctx.wait("scalar", "pe", pe6[j])
                if qi >= 2:
                    ctx.wait("scalar", "dve", x1_dve[qi - 2])
                ev_m[qi] = ctx.act(vf32("e1", TH2, c0), PSA[:, c0:c0 + TH2],
                                   AF.Identity, bias=bd("r_outb", j))
                if hf == 0:
                    ctx.wait("vector", "sd", xb_m)
                if qi >= 2:
                    ctx.wait("vector", "dve", x1_dve[qi - 2])
                if qi >= 4:
                    ctx.wait("vector", "sd", x1_st[qi - 4])
                    ctx.wait("vector", "pe", st6_m[qi - 4])
                    ctx.wait("vector", "act", sq6_m[qi - 4])
                u1 = ctx.tt(vf32("e2", TH2, c0), xbv(p)[:, h0:h0 + TH2],
                            vf32("B0", TH2, h0), ALU.mult)
                ctx.wait("vector", "dve", u1)
                u2 = ctx.tt(vf32("e2", TH2, c0), vf32("e2", TH2, c0),
                            vf32("B1", TH2, h0), ALU.subtract)
                ctx.wait("vector", "act", ev_m[qi])
                ctx.wait("vector", "dve", u2)
                x1_m = ctx.tt(vf32("x1b", TH2, (qi % 4) * TH2), vf32("e1", TH2, c0),
                              vf32("e2", TH2, c0), ALU.add)
                x1_dve[qi] = x1_m
                ctx.wait("sync", "dve", x1_m)
                x1_st[qi] = ctx.dma("sync", sX1[j * 128:(j + 1) * 128, h0:h0 + TH2],
                                    vf32("x1b", TH2, (qi % 4) * TH2), "sd")
                ctx.fence("sd")
        p6_sq(2 * DCH - 1)
        ctx.barrier()
        pool.free("wc")
        pool.free("e1")
        pool.free("e2")
        pool.free("x1b")
        pool.free("sqT")
        pool.free("GT")
        pool.free("B0")
        pool.free("B1")

        pool.alloc("rows", 3 * T * 4)
        pool.alloc("Bi2", T * 4)

        def rowT(i):
            return vf32("rows", T, i * T)[0:1, :]

        cpm = ctx.copy(rowT(0), PSB[0:1, 0:T])
        ctx.wait("vector", "dve", cpm)
        d0 = ctx.ts(rowT(0), rowT(0), 1.0 / H, None, ALU.mult)
        ctx.wait("scalar", "dve", d0)
        a1 = ctx.act(rowT(0), rowT(0), AF.Sqrt, bias=cEPS8[0:1, 0:1])
        ctx.wait("vector", "act", a1)
        d2 = ctx.recip(rowT(0), rowT(0))
        ctx.wait("tensor", "dve", d2)
        ctx.wait("vector", "dve", d2)
        bcast(rowT(0), NWR, lambda w: vf32("Bi2", NT, w * NT))
        ctx.barrier()
        pool.free("rows")

        pool.alloc("X1N", DCH * T * 2, top=True)

        def x1nv(k, w0=0, cols=None):
            return vbf("X1N", cols if cols is not None else T, k * T + w0)

        xe_m = {}
        for j in range(DCH):
            p = j % 2
            q, qk = (("sync", "sd") if p == 0 else ("gpsimd", "gd"))
            if j >= 2:
                ctx.wait(q, "act", xe_m[j - 2])
            sd_m = ctx.dma(q, xbv(p)[:, 0:T], sX1[j * 128:(j + 1) * 128, :], qk)
            ctx.fence(qk)
            ctx.wait("vector", qk, sd_m)
            t_m = ctx.tt(xbv(p)[:, 0:T], xbv(p)[:, 0:T], vf32("Bi2", T), ALU.mult)
            ctx.wait("scalar", "dve", t_m)
            xe_m[j] = ctx.act(x1nv(j), xbv(p)[:, 0:T], AF.Copy, scale=bd("r_n2w", j))
        ctx.barrier()
        pool.free("Bi2")
        pool.free("xb")
        if debug:
            for j in range(DCH):
                ctx.dma("sync", dX1N[j * 128:(j + 1) * 128, :], x1nv(j), "sd")
            ctx.barrier()

        pool.alloc("wcu1", 2 * DCH * 128 * 2)
        pool.alloc("wcu2", 2 * DCH * 128 * 2)
        pool.alloc("e1", T * 4)
        pool.alloc("e2", T * 4)
        pool.alloc("gb", 2 * T * 2)

        def wcn(nm, p, k):
            return vbf(nm, 128, p * DCH * 128 + k * 128)

        g_st, gel_m, lin1_m, gg_m7, pe7u2 = {}, {}, {}, {}, {}
        for f in range(FCH):
            p = f % 2
            g1_m = load_wcol(wT["w_up1"], f, p, DCH, "wcu1", 0,
                             pe_gate=pe7u2.get(f - 2))
            g2_m = load_wcol(wT["w_up2"], f, p, DCH, "wcu2", 1,
                             pe_gate=pe7u2.get(f - 2))
            ctx.wait("tensor", "gd", g1_m)
            ctx.wait("tensor", "gd", g2_m)
            for hf in range(2):
                qi = f * 2 + hf
                bank = (PSA, PSB)[qi % 2]
                h0 = hf * TH2
                if qi >= 2:
                    ctx.wait("tensor", "act", gel_m[qi - 2])
                    ctx.wait("tensor", "dve", lin1_m[qi - 2])
                for k in range(DCH):
                    for w in range(2):
                        ctx.mm(bank[:, w * NT:(w + 1) * NT], wcn("wcu1", p, k),
                               x1nv(k, h0 + w * NT, NT), start=(k == 0),
                               stop=(k == DCH - 1))
                pm1 = ctx.mark_last("tensor", "pe")
                for k in range(DCH):
                    for w in range(2):
                        ctx.mm(bank[:, TH2 + w * NT:TH2 + (w + 1) * NT],
                               wcn("wcu2", p, k), x1nv(k, h0 + w * NT, NT),
                               start=(k == 0), stop=(k == DCH - 1))
                pm2 = ctx.mark_last("tensor", "pe")
                pe7u2[f] = pm2
                e1h = vf32("e1", TH2, (qi % 2) * TH2)
                e2h = vf32("e2", TH2, (qi % 2) * TH2)
                ctx.wait("scalar", "pe", pm1)
                if qi >= 2:
                    ctx.wait("scalar", "dve", gg_m7[qi - 2])
                gel_m[qi] = ctx.act(e1h, bank[:, 0:TH2], AF.Gelu,
                                    bias=bias_f[:, f:f + 1])
                ctx.wait("vector", "pe", pm2)
                lin1_m[qi] = ctx.ts(e2h, bank[:, TH2:T],
                                    bias_f[:, FCH + f:FCH + f + 1], None, ALU.add)
                if qi >= 2:
                    ctx.wait("vector", "sd", g_st[qi - 2])
                ctx.wait("vector", "act", gel_m[qi])
                ctx.wait("vector", "dve", lin1_m[qi])
                gg_m7[qi] = ctx.tt(vbf("gb", TH2, (qi % 2) * T + h0), e1h, e2h,
                                   ALU.mult)
                ctx.wait("sync", "dve", gg_m7[qi])
                g_st[qi] = ctx.dma("sync", sG[f * 128:(f + 1) * 128, h0:h0 + TH2],
                                   vbf("gb", TH2, (qi % 2) * T + h0), "sd")
                ctx.fence("sd")
        ctx.barrier()
        for nm in ("wcu1", "wcu2", "e1", "e2", "gb"):
            pool.free(nm)
        pool.free("X1N")

        pool.alloc("OUT", DCH * T * 4, top=True)
        FG = c.FGRP
        NGRP = FCH // FG
        pool.alloc("wstr", 2 * FG * DCH * 128 * 2)
        pool.alloc("gstr", 2 * FG * c.THALF * 2)

        def outv(k, w0=0, cols=None):
            return vf32("OUT", cols if cols is not None else T, k * T + w0)

        wld8, grp_pe8 = {}, {}

        def p8_load(g):
            if g >= NGRP:
                return
            s = g % 2
            ctx.wait("gpsimd", "pe", grp_pe8.get(g - 2, 0))
            for i in range(FG):
                f = g * FG + i
                o = pool.off("wstr") // 2 + (s * FG + i) * DCH * 128
                ctx.dma("gpsimd", pool_bf[:, o:o + DCH * 128],
                        wT["w_down"][f * 128:(f + 1) * 128, :], "gd")
            ctx.fence("gd")
            wld8[g] = ctx.n["gd"]

        p8_load(0)
        p8_load(1)
        pe8_hist = {}
        slot_add8 = {}
        it8 = 0
        for grp in range(NGRP):
            s = grp % 2
            ctx.wait("tensor", "gd", wld8[grp])
            for half in range(T // c.THALF):
                h0 = half * c.THALF
                gp = it8 % 2
                if it8 >= 2:
                    ctx.wait("sync", "pe", pe8_hist[it8 - 2])
                for i in range(FG):
                    f = grp * FG + i
                    o = pool.off("gstr") // 2 + (gp * FG + i) * c.THALF
                    gg_m = ctx.dma("sync", pool_bf[:, o:o + c.THALF],
                                   sG[f * 128:(f + 1) * 128, h0:h0 + c.THALF], "sd")
                ctx.fence("sd")
                ctx.wait("tensor", "sd", gg_m)
                nw = c.THALF // NT
                for d in range(DCH):
                    t = (PSA, PSB)[d % 2]
                    ctx.wait("tensor", "dve", slot_add8.get(d % 2, 0))
                    for i in range(FG):
                        for w in range(nw):
                            ctx.mm(t[:, w * NT:(w + 1) * NT],
                                   vbf("wstr", 128, (s * FG + i) * DCH * 128 + d * 128),
                                   vbf("gstr", NT, (gp * FG + i) * c.THALF + w * NT),
                                   start=(i == 0), stop=(i == FG - 1))
                    pm = ctx.mark_last("tensor", "pe")
                    ctx.wait("vector", "pe", pm)
                    if grp == 0:
                        prev_add = ctx.copy(outv(d, h0, c.THALF), t[:, 0:c.THALF])
                    else:
                        prev_add = ctx.tt(outv(d, h0, c.THALF), outv(d, h0, c.THALF),
                                          t[:, 0:c.THALF], ALU.add)
                    slot_add8[d % 2] = prev_add
                pe8_hist[it8] = ctx.n["pe"]
                it8 += 1
            grp_pe8[grp] = ctx.n["pe"]
            p8_load(grp + 2)
        ctx.barrier()
        pool.free("wstr")
        pool.free("gstr")
        pool.alloc("xb", 2 * TH * 4)
        pool.alloc("sq2", 2 * T * 2)

        def sqT2(p):
            return vbf("sq2", T, p * T)

        bo_dve, st8, sq8 = {}, {}, {}

        def bo_sq(d2):
            if d2 < 0:
                return
            p2 = d2 % 2
            ctx.wait("scalar", "dve", bo_dve[d2])
            if d2 >= 2:
                ctx.wait("scalar", "pe", st8[d2 - 2])
            sq8[d2] = ctx.act(sqT2(p2), outv(d2), AF.Square)
            ctx.wait("tensor", "act", sq8[d2])
            ctx.wait("tensor", "dve", bo_dve[d2])
            for w in range(NWR):
                ctx.mm(PSA[0:1, w * NT:(w + 1) * NT], onesf[:],
                       outv(d2, w * NT, NT), start=(d2 == 0), stop=(d2 == DCH - 1))
                ctx.mm(PSB[0:1, w * NT:(w + 1) * NT], onesb[:],
                       sqT2(p2)[:, w * NT:(w + 1) * NT], start=(d2 == 0),
                       stop=(d2 == DCH - 1))
            st8[d2] = ctx.mark_last("tensor", "pe")

        for d in range(DCH):
            p = d % 2
            q, qk = (("sync", "sd") if p == 0 else ("gpsimd", "gd"))
            if d >= 2:
                ctx.wait(q, "dve", bo_dve[d - 2])
            xb_m = ctx.dma(q, xbv(p)[:, 0:T], sX1[d * 128:(d + 1) * 128, :], qk)
            ctx.fence(qk)
            a_m = ctx.act(outv(d), outv(d), AF.Identity, bias=bd("r_downb", d))
            ctx.wait("vector", "act", a_m)
            ctx.wait("vector", qk, xb_m)
            b_m = ctx.tt(outv(d), outv(d), xbv(p)[:, 0:T], ALU.add)
            bo_dve[d] = b_m
            bo_sq(d - 1)
        bo_sq(DCH - 1)
        ctx.barrier()

        pool.alloc("rows", 3 * T * 4)
        pool.alloc("Br", T * 4)
        pool.alloc("Bm", T * 4)

        def ln_rows_and_bcast(denom):
            ctx.ts(rowT(0), rowT(0), 1.0 / denom, None, ALU.mult)
            d0 = ctx.ts(rowT(1), rowT(1), 1.0 / denom, None, ALU.mult)
            ctx.wait("vector", "dve", d0)
            d1 = ctx.tt(rowT(2), rowT(0), rowT(0), ALU.mult)
            ctx.wait("vector", "dve", d1)
            d2 = ctx.tt(rowT(1), rowT(1), rowT(2), ALU.subtract)
            ctx.wait("scalar", "dve", d2)
            a1 = ctx.act(rowT(1), rowT(1), AF.Sqrt, bias=cEPS5[0:1, 0:1])
            ctx.wait("vector", "act", a1)
            d3 = ctx.recip(rowT(1), rowT(1))
            ctx.wait("vector", "dve", d3)
            ctx.wait("tensor", "dve", d3)
            bcast(rowT(1), NWR, lambda w: vf32("Br", NT, w * NT))
            d4 = ctx.tt(rowT(2), rowT(0), rowT(1), ALU.mult)
            ctx.wait("vector", "dve", d4)
            ctx.wait("tensor", "dve", d4)
            bcast(rowT(2), NWR, lambda w: vf32("Bm", NT, w * NT))

        ctx.copy(rowT(0), PSA[0:1, 0:T])
        dv = ctx.copy(rowT(1), PSB[0:1, 0:T])
        ctx.wait("vector", "dve", dv)
        ln_rows_and_bcast(D)
        ctx.barrier()

        o1_st, st9, o1_dve, sq9 = {}, {}, {}, {}

        def p9_sq(j2):
            if j2 < 0:
                return
            p2 = j2 % 2
            ctx.wait("scalar", "dve", o1_dve[j2])
            if j2 >= 2:
                ctx.wait("scalar", "pe", st9[j2 - 2])
            sq9[j2] = ctx.act(sqT2(p2), outv(j2), AF.Square)
            ctx.wait("tensor", "act", sq9[j2])
            ctx.wait("tensor", "dve", o1_dve[j2])
            for w in range(NWR):
                ctx.mm(PSA[0:1, w * NT:(w + 1) * NT], onesf[:],
                       outv(j2, w * NT, NT), start=(j2 == 0), stop=(j2 == DCH - 1))
                ctx.mm(PSB[0:1, w * NT:(w + 1) * NT], onesb[:],
                       sqT2(p2)[:, w * NT:(w + 1) * NT], start=(j2 == 0),
                       stop=(j2 == DCH - 1))
            st9[j2] = ctx.mark_last("tensor", "pe")

        for j in range(DCH):
            p = j % 2
            q, qk = (("scalar", "ad") if p == 0 else ("gpsimd", "gd"))
            if j >= 2:
                ctx.wait(q, "dve", o1_dve[j - 2])
            x_m = ctx.dma(q, xbv(p)[:, 0:T], xfm[j * 128:(j + 1) * 128, NT:TH], qk)
            ctx.fence(qk)
            t1 = ctx.tt(outv(j), outv(j), vf32("Br", T), ALU.mult)
            ctx.wait("vector", "dve", t1)
            t2 = ctx.tt(outv(j), outv(j), vf32("Bm", T), ALU.subtract)
            ctx.wait("vector", qk, x_m)
            ctx.wait("vector", "dve", t2)
            t3 = ctx.tt(outv(j), outv(j), xbv(p)[:, 0:T], ALU.add)
            o1_dve[j] = t3
            ctx.wait("sync", "dve", t3)
            o1_st[j] = ctx.dma("sync", sOUT1[j * 128:(j + 1) * 128, :], outv(j), "sd")
            ctx.fence("sd")
            p9_sq(j - 1)
        p9_sq(DCH - 1)
        ctx.wait("vector", "pe", st9[DCH - 1])
        ctx.copy(rowT(0), PSA[0:1, 0:T])
        dv = ctx.copy(rowT(1), PSB[0:1, 0:T])
        ctx.wait("vector", "dve", dv)
        ln_rows_and_bcast(D)
        ctx.barrier()
        pool.free("sq2")
        pool.free("xb")

        pool.alloc("scr", T * 4)

        def u2v(k, w0=0, cols=None):
            o = pool.off("OUT") // 2 + k * T * 2
            n = cols if cols is not None else T
            return pool_bf[:, o + w0: o + w0 + n]

        u2_m = {}
        for j in range(DCH):
            if j >= 1:
                ctx.wait("vector", "dve", u2_m[j - 1])
            t1 = ctx.tt(vf32("scr", T), outv(j), vf32("Br", T), ALU.mult)
            ctx.wait("vector", "dve", t1)
            u2_m[j] = ctx.tt(u2v(j), vf32("scr", T), vf32("Bm", T), ALU.subtract)
        ctx.barrier()
        pool.free("Br")
        pool.free("Bm")
        pool.free("rows")
        pool.free("scr")

        pool.alloc("wcf", 2 * DCH * 128 * 4)
        pool.alloc("wcs", 2 * DCH * 128 * 2)
        pool.alloc("hb2", 2 * T * 2)
        hid_st, sgn_m, pm_m, hev_m, wld = {}, {}, {}, {}, {}

        def p11_load(h):
            if h >= GCH:
                return
            p = h % 2
            o = pool.off("wcf") // 4 + p * DCH * 128
            if h >= 2:
                ctx.wait("sync", "act", sgn_m[h - 2])
            wld[h] = ctx.dma("sync", pool_f32[:, o:o + DCH * 128], wT["w_fup"][h], "sd")
            ctx.fence("sd")

        def p11_sign(h):
            if h >= GCH:
                return
            p = h % 2
            ctx.wait("scalar", "sd", wld[h])
            if h >= 2:
                ctx.wait("scalar", "pe", pm_m[h - 2])
            sgn_m[h] = ctx.act(vbf("wcs", DCH * 128, p * DCH * 128),
                               vf32("wcf", DCH * 128, p * DCH * 128), AF.Sign)

        p11_load(0)
        p11_sign(0)
        p11_load(1)
        for h in range(GCH):
            p = h % 2
            ctx.wait("tensor", "act", sgn_m[h])
            t = (PSA, PSB)[p]
            if h >= 2:
                ctx.wait("tensor", "act", hev_m[h - 2])
            for k in range(DCH):
                for w in range(NWR):
                    ctx.mm(t[:, w * NT:(w + 1) * NT],
                           vbf("wcs", 128, p * DCH * 128 + k * 128),
                           u2v(k, w * NT, NT), start=(k == 0), stop=(k == DCH - 1))
            pm_m[h] = ctx.mark_last("tensor", "pe")
            p11_sign(h + 1)
            p11_load(h + 2)
            ctx.wait("scalar", "pe", pm_m[h])
            if h >= 2:
                ctx.wait("scalar", "sd", hid_st[h - 2])
            hev_m[h] = ctx.act(vbf("hb2", T, p * T), t[:, 0:T], AF.Relu,
                               scale=bias_g[:, h:h + 1])
            ctx.wait("sync", "act", hev_m[h])
            hid_st[h] = ctx.dma("sync", sHID[h * 128:(h + 1) * 128, :],
                                vbf("hb2", T, p * T), "sd")
            ctx.fence("sd")
        ctx.barrier()
        for nm in ("wcf", "wcs", "hb2", "OUT"):
            pool.free(nm)

        pool.alloc("OUT", DCH * T * 4, top=True)
        HG = c.HGRP
        NG2 = GCH // HG
        TQ12 = NT
        pool.alloc("wstr", 2 * HG * DCH * 128 * 2)
        pool.alloc("wstg", 2 * DCH * 128 * 4)
        pool.alloc("hstr", 2 * HG * TQ12 * 2)

        wld12, sgn12, grp_pe12 = {}, {}, {}
        nls = [0, 0]

        def p12_load():
            hh = nls[0]
            if hh >= GCH:
                return
            s = hh % 2
            ctx.wait("sync", "act", sgn12.get(hh - 2, 0))
            wld12[hh] = ctx.dma("sync", vf32("wstg", DCH * 128, s * DCH * 128),
                                wT["w_fdn"][hh * 128:(hh + 1) * 128, :], "sd")
            ctx.fence("sd")
            nls[0] += 1

        def p12_sign():
            hh = nls[1]
            if hh >= GCH:
                return
            while nls[0] < min(hh + 2, GCH):
                p12_load()
            g, i = hh // HG, hh % HG
            s2 = g % 2
            ctx.wait("scalar", "sd", wld12[hh])
            ctx.wait("scalar", "pe", grp_pe12.get(g - 2, 0))
            sgn12[hh] = ctx.act(vbf("wstr", DCH * 128, (s2 * HG + i) * DCH * 128),
                                vf32("wstg", DCH * 128, (hh % 2) * DCH * 128), AF.Sign)
            nls[1] += 1

        p12_load()
        p12_load()
        for _ in range(HG):
            p12_sign()
        pe12_hist = {}
        slot_add12 = {}
        it12 = 0
        for grp in range(NG2):
            s2 = grp % 2
            ctx.wait("tensor", "act", sgn12[grp * HG + HG - 1])
            for q in range(T // TQ12):
                h0 = q * TQ12
                gp = it12 % 2
                if it12 >= 2:
                    ctx.wait("sync", "pe", pe12_hist[it12 - 2])
                for i in range(HG):
                    hh = grp * HG + i
                    o = pool.off("hstr") // 2 + (gp * HG + i) * TQ12
                    hh_m = ctx.dma("sync", pool_bf[:, o:o + TQ12],
                                   sHID[hh * 128:(hh + 1) * 128, h0:h0 + TQ12], "sd")
                ctx.fence("sd")
                ctx.wait("tensor", "sd", hh_m)
                for d in range(DCH):
                    t = (PSA, PSB)[d % 2]
                    ctx.wait("tensor", "dve", slot_add12.get(d % 2, 0))
                    for i in range(HG):
                        ctx.mm(t[:, 0:TQ12],
                               vbf("wstr", 128, (s2 * HG + i) * DCH * 128 + d * 128),
                               vbf("hstr", TQ12, (gp * HG + i) * TQ12),
                               start=(i == 0), stop=(i == HG - 1))
                    pm = ctx.mark_last("tensor", "pe")
                    ctx.wait("vector", "pe", pm)
                    if grp == 0:
                        prev_add = ctx.copy(outv(d, h0, TQ12), t[:, 0:TQ12])
                    else:
                        prev_add = ctx.tt(outv(d, h0, TQ12), outv(d, h0, TQ12),
                                          t[:, 0:TQ12], ALU.add)
                    slot_add12[d % 2] = prev_add
                pe12_hist[it12] = ctx.n["pe"]
                it12 += 1
                p12_sign()
            grp_pe12[grp] = ctx.n["pe"]

        ctx.barrier()
        pool.free("wstr")
        pool.free("wstg")
        pool.free("hstr")
        pool.alloc("rows", 3 * T * 4)
        pool.alloc("sq2", 2 * T * 2)
        pool.alloc("Br", T * 4)
        pool.alloc("Bm", T * 4)
        stf, sqf, fo_m = {}, {}, {}
        for d in range(DCH):
            p = d % 2
            if d >= 2:
                ctx.wait("scalar", "pe", stf[d - 2])
                ctx.wait("vector", "pe", stf[d - 2])
                ctx.wait("vector", "act", sqf[d - 2])
            f_m = ctx.ts(outv(d), outv(d), dsc2[:, d:d + 1], None, ALU.mult)
            fo_m[d] = f_m
            ctx.wait("scalar", "dve", f_m)
            sqf[d] = ctx.act(sqT2(p), outv(d), AF.Square)
            ctx.wait("tensor", "act", sqf[d])
            ctx.wait("tensor", "dve", f_m)
            for w in range(NWR):
                ctx.mm(PSA[0:1, w * NT:(w + 1) * NT], onesf[:], outv(d, w * NT, NT),
                       start=(d == 0), stop=(d == DCH - 1))
                ctx.mm(PSB[0:1, w * NT:(w + 1) * NT], onesb[:],
                       sqT2(p)[:, w * NT:(w + 1) * NT], start=(d == 0),
                       stop=(d == DCH - 1))
            stf[d] = ctx.mark_last("tensor", "pe")
        ctx.wait("vector", "pe", stf[DCH - 1])
        ctx.copy(rowT(0), PSA[0:1, 0:T])
        dv = ctx.copy(rowT(1), PSB[0:1, 0:T])
        ctx.wait("vector", "dve", dv)
        ln_rows_and_bcast(D)
        ctx.barrier()
        pool.free("rows")
        pool.free("sq2")

        pool.alloc("o1", 2 * T * 4)
        pool.alloc("yb", 2 * T * 4)
        y_st, y_dve = {}, {}
        for j in range(DCH):
            p = j % 2
            if j >= 2:
                ctx.wait("vector", "sd", y_st[j - 2])
                ctx.wait("scalar", "dve", y_dve[j - 2])
            sd_m = ctx.dma("scalar", vf32("o1", T, p * T),
                           sOUT1[j * 128:(j + 1) * 128, :], "ad")
            ctx.fence("ad")
            t1 = ctx.tt(vf32("yb", T, p * T), outv(j), vf32("Br", T), ALU.mult)
            ctx.wait("vector", "dve", t1)
            t2 = ctx.tt(vf32("yb", T, p * T), vf32("yb", T, p * T), vf32("Bm", T),
                        ALU.subtract)
            ctx.wait("vector", "ad", sd_m)
            ctx.wait("vector", "dve", t2)
            t3 = ctx.tt(vf32("yb", T, p * T), vf32("yb", T, p * T), vf32("o1", T, p * T),
                        ALU.add)
            y_dve[j] = t3
            ctx.wait("sync", "dve", t3)
            y_st[j] = ctx.dma("sync", yfm[j * 128:(j + 1) * 128, :], vf32("yb", T, p * T),
                              "sd")
            ctx.fence("sd")
        ctx.fence("sd")
        ctx.fence("ad")
        ctx.fence("gd")
        ctx.wait("sync", "sd", ctx.n["sd"])
        ctx.wait("sync", "gd", ctx.n["gd"])
        ctx.wait("sync", "ad", ctx.n["ad"])

        block = E(nc.Block())

        @block.sync
        def _(eng):
            for fn in ctx.progs["sync"]:
                fn(eng)

        @block.gpsimd
        def _(eng):
            for fn in ctx.progs["gpsimd"]:
                fn(eng)

        @block.vector
        def _(eng):
            for fn in ctx.progs["vector"]:
                fn(eng)

        @block.scalar
        def _(eng):
            for fn in ctx.progs["scalar"]:
                fn(eng)

        @block.tensor
        def _(eng):
            for fn in ctx.progs["tensor"]:
                fn(eng)

    return nc


def make_core_inputs(c: CFG, inp, nseq_chunks=4):
    D, T, NT = c.D, c.T, c.NT
    x = np.ascontiguousarray(np.asarray(inp["L0_attn_x"], dtype=np.float32)).reshape(-1, D)
    assert x.shape[0] == NC * T
    def tile_w(a):
        a = np.asarray(a, np.float32)
        Kd, Jd = a.shape
        return np.ascontiguousarray(
            a.reshape(Kd // 128, 128, Jd // 128, 128).transpose(2, 1, 0, 3)
            .reshape(Jd // 128, 128, Kd))

    shared = {
        "w_left": tile_w(inp["L0_attn_left_W"]),
        "w_right": tile_w(inp["L0_attn_right_W"]),
        "w_ga": tile_w(inp["L0_attn_ga_W"]),
        "w_gx": tile_w(inp["L0_attn_gx_W"]),
        "w_out": tile_w(inp["L0_attn_out_W"]),
        "w_up1": tile_w(inp["L0_attn_up1_W"]),
        "w_up2": tile_w(inp["L0_attn_up2_W"]),
        "w_fup": tile_w(np.asarray(inp["L0_ffn_up_weight"], np.float32).T),
        "w_down": np.ascontiguousarray(np.asarray(inp["L0_attn_down_W"], np.float32)),
        "w_fdn": np.ascontiguousarray(np.asarray(inp["L0_ffn_down_weight"], np.float32).T),
        "r_n1w": np.asarray(inp["L0_attn_norm1_w"], np.float32).reshape(1, -1),
        "r_leftb": np.asarray(inp["L0_attn_left_b"], np.float32).reshape(1, -1),
        "r_rightb": np.asarray(inp["L0_attn_right_b"], np.float32).reshape(1, -1),
        "r_convb": np.asarray(inp["L0_attn_conv_b"], np.float32).reshape(1, -1),
        "r_convw": np.ascontiguousarray(np.asarray(inp["L0_attn_conv_w"], np.float32)),
        "r_gab": np.asarray(inp["L0_attn_ga_b"], np.float32).reshape(1, -1),
        "r_gxb": np.asarray(inp["L0_attn_gx_b"], np.float32).reshape(1, -1),
        "r_apar": np.asarray(inp["L0_attn_a_param"], np.float32).reshape(1, -1),
        "r_outb": np.asarray(inp["L0_attn_out_b"], np.float32).reshape(1, -1),
        "r_n2w": np.asarray(inp["L0_attn_norm2_w"], np.float32).reshape(1, -1),
        "r_up1b": np.asarray(inp["L0_attn_up1_b"], np.float32).reshape(1, -1),
        "r_up2b": np.asarray(inp["L0_attn_up2_b"], np.float32).reshape(1, -1),
        "r_downb": np.asarray(inp["L0_attn_down_b"], np.float32).reshape(1, -1),
        "r_uscale": np.asarray(inp["L0_ffn_up_scale"], np.float32).reshape(1, -1),
        "r_dscale": np.asarray(inp["L0_ffn_down_scale"], np.float32).reshape(1, -1),
        "r_oscale": np.asarray(inp["L0_ffn_output_scale"], np.float32).reshape(1, -1),
    }
    maps = []
    for core in range(NC):
        t0 = core * T
        j = core % nseq_chunks
        halo = np.zeros((NT, D), np.float32)
        if j != 0:
            halo[:] = x[t0 - NT:t0]
        blk = np.concatenate([halo, x[t0:t0 + T]], 0)
        m = dict(shared)
        m["xfm"] = np.ascontiguousarray(blk.T)
        m["halo_mask"] = np.full((128, 1), 0.0 if j == 0 else 1.0, np.float32)
        mg = np.zeros(NC, np.float32)
        lg = np.zeros(NC, np.float32)
        s0 = (core // nseq_chunks) * nseq_chunks
        mg[s0:core] = 1.0
        if j > 0:
            lg[core - 1] = 1.0
        m["scanm"] = np.ascontiguousarray(np.broadcast_to(mg, (128, NC)))
        m["scanl"] = np.ascontiguousarray(np.broadcast_to(lg, (128, NC)))
        maps.append(m)
    return maps


def kernel(**inputs):
    c = CFG_FULL
    nc = build_nc(c, debug=False)
    maps = make_core_inputs(c, inputs)
    res = run_bass_kernel_spmd(nc, maps, core_ids=list(range(NC)))
    outs = [np.asarray(res.results[i]["yfm"]).T for i in range(NC)]
    y = np.concatenate(outs, 0).reshape(2, NC * c.T // 2, c.D)
    return np.ascontiguousarray(y.astype(np.float32))
